# version 09
# baseline (speedup 1.0000x reference)
import numpy as np
import ml_dtypes
import concourse.bass as bass
import concourse.mybir as mybir
from concourse.bass_utils import run_bass_kernel_spmd
from contextlib import ExitStack

F32 = mybir.dt.float32
BF16 = mybir.dt.bfloat16
I16 = mybir.dt.int16
AF = mybir.ActivationFunctionType
ALU = mybir.AluOpType
AX = mybir.AxisListType

B, S, D, H = 2, 4096, 2048, 8192
NKV, DH, NHEAD = 4, 128, 16
SCALE = float(DH) ** -0.5
TOK = 1024
SG = S // 512
LN127 = float(np.log(127.0))
EPS1, EPS2 = 1e-5, 1e-8


class Ctr:
    def __init__(self, st, nc, name):
        self.h = st.enter_context(nc.semaphore(name))
        self.n = 0


def build(phases=("PA", "PBC", "PE"), debug=()):
    nc = bass.Bass("TRN2", target_bir_lowering=False)
    ES = {e: [] for e in ("tensor", "vector", "scalar", "gpsimd", "sync")}

    def em(eng):
        def reg(f):
            ES[eng].append(f)
            return f
        return reg

    x_full = nc.dram_tensor("x_full", [S, D], F32, kind="ExternalInput").ap()
    x_own = nc.dram_tensor("x_own", [TOK, D], F32, kind="ExternalInput").ap()
    qpos_t = nc.dram_tensor("qpos", [1, TOK], F32, kind="ExternalInput").ap()
    wq_t = nc.dram_tensor("wq", [D, D], F32, kind="ExternalInput").ap()
    wk_t = nc.dram_tensor("wk", [D, 512], F32, kind="ExternalInput").ap()
    wv_t = nc.dram_tensor("wv", [D, 512], F32, kind="ExternalInput").ap()
    wo_t = nc.dram_tensor("wo", [D, D], F32, kind="ExternalInput").ap()
    wg_t = nc.dram_tensor("wg", [H, D], F32, kind="ExternalInput").ap()
    wu_t = nc.dram_tensor("wu", [H, D], F32, kind="ExternalInput").ap()
    wd_t = nc.dram_tensor("wd", [D, H], F32, kind="ExternalInput").ap()
    out_t = nc.dram_tensor("out", [TOK, D], F32, kind="ExternalOutput").ap()
    scrg = nc.dram_tensor("scrg", [H, D], BF16, kind="Internal").ap()
    scru = nc.dram_tensor("scru", [H, D], BF16, kind="Internal").ap()
    scrd = nc.dram_tensor("scrd", [D, H], BF16, kind="Internal").ap()
    scrr = nc.dram_tensor("scrr", [TOK, D], F32, kind="Internal").ap()

    dbg = {}
    dbg_specs = {
        "d_kT": ([128, NKV * S], BF16), "d_v": ([128, 32 * 512], BF16),
        "d_qT": ([128, NHEAD * 512], BF16), "d_resid": ([128, 8 * D], F32),
        "d_xqT": ([128, 16 * TOK], BF16), "d_pT": ([128, 64 * 512], BF16),
        "d_aoT": ([128, NHEAD * 512], BF16), "d_h1T": ([128, 16 * 512], BF16),
        "d_sml": ([128, 128], F32), "d_h1b": ([128, 2 * 2048], BF16),
        "d_xst": ([128, 2 * 2048], F32), "d_spos": ([128, 32], F32),
        "d_qposB": ([128, 512], F32), "d_mask": ([128, 512], BF16),
    }
    for name in debug:
        shape, dt = dbg_specs[name]
        dbg[name] = nc.dram_tensor(name, shape, dt, kind="ExternalOutput").ap()

    st = ExitStack()
    with st:
        ones128 = st.enter_context(nc.sbuf_tensor("ones128", [128, 128], BF16))
        identT = st.enter_context(nc.sbuf_tensor("identT", [128, 128], BF16))
        onescolf = st.enter_context(nc.sbuf_tensor("onescolf", [1, 128], F32))
        sposall = st.enter_context(nc.sbuf_tensor("sposall", [128, 32], F32))
        sml = st.enter_context(nc.sbuf_tensor("smalls", [128, 128], F32))
        qpr = st.enter_context(nc.sbuf_tensor("qpr", [1, 256], F32))

        MBASE = (nc.sbuf_top - 208896) & ~31
        assert MBASE >= nc.sbuf_base + 256, f"auto region too small: {nc.sbuf_base} vs {MBASE}"

        def at(name, shape, dtype, off):
            return nc.alloc_sbuf_tensor_at(name, shape, dtype, offset=MBASE + off)

        RA, RBK, RBV, RC, RD = 0, 65536, 98304, 131072, 163840
        resid = at("resid", [128, 16384], F32, RA)
        pT = at("pT", [128, 32768], BF16, RA)
        kT = at("kT", [128, 16384], BF16, RBK)
        wg3 = at("wg3", [128, 16, 1024], BF16, RBK)
        wd3k = at("wd3k", [128, 32, 512], BF16, RBK)
        vtm = at("vtm", [128, 16384], BF16, RBV)
        wu3 = at("wu3", [128, 16, 1024], BF16, RBV)
        wd3v = at("wd3v", [128, 32, 512], BF16, RBV)
        wkv = at("wkv", [128, 16384], BF16, RC)
        wpr3 = at("wpr3", [128, 2, 16, 512], BF16, RC)
        xqT3 = at("xqT3", [128, 16, 1024], BF16, RC)
        xqTf = at("xqTf", [128, 16384], BF16, RC)
        xst = [at("xst0", [128, 2048], F32, RD + 0)[:, :], at("xst1", [128, 2048], F32, RD + 8192)[:, :]]
        h1b = [at("h1b0", [128, 2048], BF16, RD + 16384)[:, :],
               at("h1b1", [128, 2048], BF16, RD + 20480)[:, :]]
        h1T3 = at("h1T3", [128, 16, 512], BF16, RD + 24576)
        h1Tb = at("h1Tb", [128, 8192], BF16, RD + 24576)
        pin = [at("pin0", [128, 1024], BF16, RD + 40960)[:, :],
               at("pin1", [128, 1024], BF16, RD + 43008)[:, :]]
        qT3 = at("qT3", [128, 16, 512], BF16, RD + 0)
        qTb = at("qTb", [128, 8192], BF16, RD + 0)
        expw = [at("expw0", [128, 1024], BF16, RD + 16384),
                at("expw1", [128, 1024], BF16, RD + 18432)]
        expb = [[expw[p][:, h * 512:(h + 1) * 512] for h in range(2)] for p in range(2)]
        maskb = [at("maskb0", [128, 512], BF16, RD + 20480)[:, :]]
        sqb = at("sqb", [128, 512], BF16, RD + 21504)[:, :]
        recipb = [at("recip0", [128, 512], F32, RD + 16384)[:, :],
                  at("recip1", [128, 512], F32, RD + 18432)[:, :]]
        qposB = at("qposB", [128, 512], F32, RD + 22528)[:, :]
        aoT3 = h1T3
        h2s = at("h2s", [128, 2048], F32, RD + 0)[:, :]
        h3s = at("h3s", [128, 2048], F32, RD + 8192)[:, :]
        i16s = at("i16s", [128, 2048], I16, RD + 16384)[:, :]
        ifs = at("ifs", [128, 2048], F32, RD + 20480)[:, :]
        xqb = at("xqb", [128, 2048], BF16, RD + 28672)[:, :]
        resst = at("resst", [128, 8192], F32, RD + 0)
        outst = [at("outst0", [128, 512], F32, RD + 32768)[:, :],
                 at("outst1", [128, 512], F32, RD + 34816)[:, :]]
        gsb = [at("gs0", [128, 512], F32, RD + 36864)[:, :],
               at("gs1", [128, 512], F32, RD + 38912)[:, :]]

        psw = st.enter_context(nc.psum_tensor("psw", [128, 1024], F32))
        ps_tr = [st.enter_context(nc.psum_tensor(f"pstr{i}", [128, 512], BF16)) for i in range(2)]
        psf = [psw[:, 0:512], psw[:, 512:1024]] + \
            [st.enter_context(nc.psum_tensor(f"psf{i}", [128, 512], F32)) for i in range(4)]

        dX = [Ctr(st, nc, "dX0"), Ctr(st, nc, "dX1")]
        dKV = Ctr(st, nc, "dKV")
        dPin = [Ctr(st, nc, "dPin0"), Ctr(st, nc, "dPin1")]
        dQp = Ctr(st, nc, "dQp")
        dWp = [Ctr(st, nc, "dWp0"), Ctr(st, nc, "dWp1")]
        dSC = [Ctr(st, nc, "dSC0"), Ctr(st, nc, "dSC1")]
        dTR = [Ctr(st, nc, "dTR0"), Ctr(st, nc, "dTR1")]
        dRes = Ctr(st, nc, "dRes")
        dEv = Ctr(st, nc, "dEv")
        dOutS = [Ctr(st, nc, "dOut0"), Ctr(st, nc, "dOut1")]
        dDbg = Ctr(st, nc, "dDbg")
        tS = Ctr(st, nc, "tS")
        vS = Ctr(st, nc, "vS")
        cS = Ctr(st, nc, "cS")
        gS = Ctr(st, nc, "gS")
        bar = Ctr(st, nc, "bar")
        nbar = [0]

        def barrier():
            nbar[0] += 1
            tgt = 5 * nbar[0]
            gsems = [(c.h, c.n) for c in [dKV, dPin[0], dPin[1], dWp[0], dWp[1], dEv,
                                          dOutS[0], dOutS[1]]]
            ssems = [(c.h, c.n) for c in [dX[0], dX[1], dQp, dTR[0], dTR[1], dRes, dDbg]]
            csems = [(dSC[0].h, dSC[0].n), (dSC[1].h, dSC[1].n)]

            @em("tensor")
            def _(t, tgt=tgt):
                t.drain()
                t.sem_inc(bar.h, 1)
                t.wait_ge(bar.h, tgt)

            @em("vector")
            def _(v, tgt=tgt):
                v.drain()
                v.sem_inc(bar.h, 1)
                v.wait_ge(bar.h, tgt)

            @em("scalar")
            def _(c, tgt=tgt, ws=tuple(csems)):
                c.drain()
                for h, n in ws:
                    c.wait_ge(h, n)
                c.sem_inc(bar.h, 1)
                c.wait_ge(bar.h, tgt)

            @em("gpsimd")
            def _(g, tgt=tgt, ws=tuple(gsems)):
                for h, n in ws:
                    g.wait_ge(h, n)
                g.sem_inc(bar.h, 1)
                g.wait_ge(bar.h, tgt)

            @em("sync")
            def _(s, tgt=tgt, ws=tuple(ssems)):
                for h, n in ws:
                    s.wait_ge(h, n)
                s.sem_inc(bar.h, 1)
                s.wait_ge(bar.h, tgt)
            bar.n = tgt

        @em("gpsimd")
        def _(g):
            g.memset(ones128[:, :], 1.0).then_inc(gS.h, 1)
            g.memset(identT[:, :], 1.0).then_inc(gS.h, 1)
            g.memset(onescolf[0:1, :], 1.0).then_inc(gS.h, 1)
            g.memset(sml[:, 120:121], EPS1).then_inc(gS.h, 1)
            g.memset(sml[:, 121:122], EPS2).then_inc(gS.h, 1)
            g.memset(sml[:, 122:123], LN127).then_inc(gS.h, 1)
            g.memset(sml[:, 123:124], -LN127).then_inc(gS.h, 1)
            g.iota(sposall[:, :], [[128, 32]], channel_multiplier=1,
                   allow_small_or_imprecise_dtypes=True).then_inc(gS.h, 1)
            g.wait_ge(gS.h, 8)
            g.affine_select(out=identT[:, :], in_=identT[:, :], compare_op=ALU.is_equal,
                            fill=0.0, base=0, pattern=[[-1, 128]],
                            channel_multiplier=1).then_inc(gS.h, 1)
            g.wait_ge(gS.h, 9)
        gS.n += 9

        @em("scalar")
        def _(c):
            c.add_instruction(mybir.InstLoadActFuncSet(
                name=nc.get_next_instruction_name(), act_func_set_id=6, ins=[], outs=[]))
        barrier()

        prep_list = []
        for blk in range(H // 128):
            for half in range(2):
                prep_list.append((wg_t, scrg, blk, half))
        for blk in range(H // 128):
            for half in range(2):
                prep_list.append((wu_t, scru, blk, half))
        for blk in range(D // 128):
            for part in range(8):
                prep_list.append((wd_t, scrd, blk, part))
        NPREP = len(prep_list)
        pst = {"i": 0, "free": [0, 0]}

        def emit_prep(n):
            cnt = 0
            while cnt < n and pst["i"] < NPREP:
                i = pst["i"]
                pst["i"] += 1
                cnt += 1
                src_t, dst_t, blk, part = prep_list[i]
                sl = i % 2
                tile = pin[sl]
                freev = pst["free"][sl]

                @em("gpsimd")
                def _(g, src_t=src_t, blk=blk, part=part, tile=tile, freev=freev, sl=sl):
                    g.wait_ge(dSC[sl].h, freev)
                    g.dma_start(tile, src_t[blk * 128:(blk + 1) * 128,
                                            part * 1024:(part + 1) * 1024]).then_inc(dPin[sl].h, 16)
                dPin[sl].n += 16
                inv = dPin[sl].n

                @em("scalar")
                def _(c, tile=tile, inv=inv, sl=sl):
                    c.wait_ge(dPin[sl].h, inv)
                    c.activation(tile, tile, AF.Sign).then_inc(cS.h, 1)
                cS.n += 1
                sgv = cS.n
                dSC[sl].n += 16

                @em("scalar")
                def _(c, dst_t=dst_t, blk=blk, part=part, tile=tile, sgv=sgv, sl=sl):
                    c.wait_ge(cS.h, sgv)
                    c.dma_start(dst_t[blk * 128:(blk + 1) * 128,
                                      part * 1024:(part + 1) * 1024], tile).then_inc(dSC[sl].h, 16)
                pst["free"][sl] = dSC[sl].n

        def emit_ln(src_ap, dst_ap, eps, rdy, gate, par):
            b0 = par * 16

            @em("scalar")
            def _(c, src_ap=src_ap, dst_ap=dst_ap, rdy=rdy, gate=gate, b0=b0, base0=cS.n):
                if rdy is not None:
                    c.wait_ge(rdy[0].h, rdy[1])
                if gate is not None:
                    c.wait_ge(gate[0].h, gate[1])
                c.activation(dst_ap, src_ap, AF.Square,
                             accum_out=sml[:, b0 + 1:b0 + 2]).then_inc(cS.h, 1)
                c.wait_ge(cS.h, base0 + 1)
                c.activation(dst_ap, src_ap, AF.Copy,
                             accum_out=sml[:, b0 + 0:b0 + 1]).then_inc(cS.h, 1)
            cS.n += 2
            st1 = cS.n

            @em("vector")
            def _(v, st1=st1, b0=b0, base=vS.n):
                v.wait_ge(cS.h, st1)
                v.tensor_single_scalar(sml[:, b0 + 2:b0 + 3], sml[:, b0 + 0:b0 + 1],
                                       1.0 / D, ALU.mult)
                v.tensor_single_scalar(sml[:, b0 + 4:b0 + 5], sml[:, b0 + 1:b0 + 2],
                                       1.0 / D, ALU.mult).then_inc(vS.h, 1)
                v.wait_ge(vS.h, base + 1)
                v.tensor_scalar(sml[:, b0 + 3:b0 + 4], sml[:, b0 + 2:b0 + 3],
                                sml[:, b0 + 2:b0 + 3], -1.0, ALU.mult,
                                ALU.mult).then_inc(vS.h, 1)
                v.wait_ge(vS.h, base + 2)
                v.tensor_scalar(sml[:, b0 + 3:b0 + 4], sml[:, b0 + 4:b0 + 5],
                                sml[:, b0 + 3:b0 + 4], None, ALU.add,
                                ALU.bypass).then_inc(vS.h, 1)
            vS.n += 3
            stv = vS.n

            @em("scalar")
            def _(c, stv=stv, b0=b0, eps=eps, base=cS.n):
                c.wait_ge(vS.h, stv)
                c.activation(sml[:, b0 + 4:b0 + 5], sml[:, b0 + 3:b0 + 4], AF.Ln,
                             bias=sml[:, eps:eps + 1]).then_inc(cS.h, 1)
                c.wait_ge(cS.h, base + 1)
                c.activation(sml[:, b0 + 5:b0 + 6], sml[:, b0 + 4:b0 + 5], AF.Exp,
                             scale=-0.5).then_inc(cS.h, 1)
            cS.n += 2
            rsv = cS.n

            @em("vector")
            def _(v, rsv=rsv, b0=b0):
                v.wait_ge(cS.h, rsv)
                v.tensor_scalar(sml[:, b0 + 6:b0 + 7], sml[:, b0 + 2:b0 + 3],
                                sml[:, b0 + 5:b0 + 6], -1.0, ALU.mult,
                                ALU.mult).then_inc(vS.h, 1)
            vS.n += 1
            nmv = vS.n

            @em("scalar")
            def _(c, src_ap=src_ap, dst_ap=dst_ap, nmv=nmv, b0=b0):
                c.wait_ge(vS.h, nmv)
                c.activation(dst_ap, src_ap, AF.Identity, scale=sml[:, b0 + 5:b0 + 6],
                             bias=sml[:, b0 + 6:b0 + 7]).then_inc(cS.h, 1)
            cS.n += 1
            return cS.n

        trps_free = [0, 0]

        def emit_transpose_tile(src_bf_ap, dst3, tl, h1v, extra_tgate=0):
            for c4 in range(4):
                pfv = trps_free[c4 % 2]

                @em("tensor")
                def _(t, src=src_bf_ap, c4=c4, h1v=h1v, pfv=pfv, eg=extra_tgate):
                    t.wait_ge(cS.h, h1v)
                    t.wait_ge(vS.h, pfv)
                    if eg:
                        t.wait_ge(tS.h, eg)
                    for cc in range(4):
                        c_ = c4 * 4 + cc
                        ins = t.transpose(ps_tr[c4 % 2][:, cc * 128:(cc + 1) * 128],
                                          src[:, c_ * 128:(c_ + 1) * 128], identT[:, :])
                    ins.then_inc(tS.h, 1)
                tS.n += 1
                trv = tS.n

                @em("vector")
                def _(v, c4=c4, tl=tl, trv=trv, dst3=dst3):
                    v.wait_ge(tS.h, trv)
                    v.tensor_copy(dst3[:, c4 * 4:(c4 + 1) * 4, tl * 128:(tl + 1) * 128],
                                  ps_tr[c4 % 2][:, :]).then_inc(vS.h, 1)
                vS.n += 1
                trps_free[c4 % 2] = vS.n
            return tS.n, vS.n

        psfree = [0] * 6

        def emit_mm_evac(lhs_slices, rhs_slices, pb, evac_fn, waits=()):
            @em("tensor")
            def _(t, L=lhs_slices, R=rhs_slices, pb=pb, pf=psfree[pb], waits=tuple(waits)):
                for ctr, val in waits:
                    t.wait_ge(ctr.h, val)
                t.wait_ge(vS.h, pf)
                n = len(L)
                for i_ in range(n):
                    ins = t.matmul(psf[pb][:, :], L[i_], R[i_],
                                   start=(i_ == 0), stop=(i_ == n - 1))
                ins.then_inc(tS.h, 1)
            tS.n += 1
            mmv = tS.n

            @em("vector")
            def _(v, pb=pb, mmv=mmv, evac_fn=evac_fn):
                v.wait_ge(tS.h, mmv)
                evac_fn(v, psf[pb]).then_inc(vS.h, 1)
            vS.n += 1
            psfree[pb] = vS.n
            return vS.n

        kmax_col = 72
        if "PA" in phases:
            @em("gpsimd")
            def _(g):
                for c_ in range(16):
                    g.dma_start(wkv[:, c_ * 1024:c_ * 1024 + 512],
                                wk_t[c_ * 128:(c_ + 1) * 128, :]).then_inc(dKV.h, 16)
                    g.dma_start(wkv[:, c_ * 1024 + 512:(c_ + 1) * 1024],
                                wv_t[c_ * 128:(c_ + 1) * 128, :]).then_inc(dKV.h, 16)
            dKV.n += 512
            wkv_rdy = dKV.n

            xfree = [0, 0]
            h1free = [0, 0]
            kv_gate = 0
            for grp in range(SG):
                emit_prep(12)
                for tl in range(4):
                    ti = grp * 4 + tl
                    sl = ti % 2

                    @em("sync")
                    def _(s, ti=ti, sl=sl, fv=xfree[sl]):
                        s.wait_ge(cS.h, fv)
                        s.dma_start(xst[sl],
                                    x_full[ti * 128:(ti + 1) * 128, :]).then_inc(dX[sl].h, 16)
                    dX[sl].n += 16
                    h1v = emit_ln(xst[sl], h1b[sl], 120, (dX[sl], dX[sl].n),
                                  (tS, h1free[sl]), sl)
                    xfree[sl] = h1v
                    trv, _ = emit_transpose_tile(h1b[sl], h1T3, tl, h1v,
                                                 extra_tgate=(kv_gate if tl == 0 else 0))
                    h1free[sl] = trv
                kvw = vS.n
                for k_ in range(NKV):
                    pb = k_ % 2
                    L = [wkv[:, c_ * 1024 + k_ * 128:c_ * 1024 + (k_ + 1) * 128]
                         for c_ in range(16)]
                    R = [h1T3[:, c_, :] for c_ in range(16)]

                    def ev(v, ps, k_=k_, grp=grp):
                        return v.tensor_copy(
                            kT[:, k_ * S + grp * 512:k_ * S + (grp + 1) * 512], ps[:, :])
                    emit_mm_evac(L, R, pb, ev, waits=[(vS, kvw), (dKV, wkv_rdy)])
                for tsub in range(4):
                    pb = tsub % 2
                    L = [h1T3[:, c_, tsub * 128:(tsub + 1) * 128] for c_ in range(16)]
                    R = [wkv[:, c_ * 1024 + 512:(c_ + 1) * 1024] for c_ in range(16)]
                    stt = grp * 4 + tsub

                    def ev(v, ps, stt=stt):
                        return v.tensor_copy(vtm[:, stt * 512:(stt + 1) * 512], ps[:, :])
                    emit_mm_evac(L, R, pb, ev, waits=[(vS, kvw), (dKV, wkv_rdy)])
                kv_gate = tS.n

            kT_done = vS.n
            for k_ in range(NKV):
                for ch in range(8):
                    idx = k_ * 8 + ch

                    @em("scalar")
                    def _(c, k_=k_, ch=ch, w=kT_done, sq_gate=tS.n):
                        c.wait_ge(vS.h, w)
                        c.wait_ge(tS.h, sq_gate)
                        c.activation(sqb, kT[:, k_ * S + ch * 512:k_ * S + (ch + 1) * 512],
                                     AF.Square).then_inc(cS.h, 1)
                    cS.n += 1
                    sqv = cS.n

                    @em("tensor")
                    def _(t, sqv=sqv, pf=psfree[2]):
                        t.wait_ge(cS.h, sqv)
                        t.wait_ge(vS.h, pf)
                        t.matmul(psf[2][:, :], ones128[:, :], sqb, start=True,
                                 stop=True).then_inc(tS.h, 1)
                    tS.n += 1
                    mmv = tS.n

                    @em("vector")
                    def _(v, mmv=mmv, idx=idx):
                        v.wait_ge(tS.h, mmv)
                        v.reduce_max(sml[0:1, 40 + idx:41 + idx], psf[2][0:1, :],
                                     axis=AX.X).then_inc(vS.h, 1)
                    vS.n += 1
                    psfree[2] = vS.n

            @em("vector")
            def _(v, w=vS.n):
                v.wait_ge(vS.h, w)
                v.reduce_max(sml[0:1, kmax_col:kmax_col + 1], sml[0:1, 40:72],
                             axis=AX.X).then_inc(vS.h, 1)
            vS.n += 1

            if "d_kT" in dbg:
                @em("sync")
                def _(s, w=vS.n):
                    s.wait_ge(vS.h, w)
                    s.dma_start(dbg["d_kT"], kT[:, :]).then_inc(dDbg.h, 16)
                dDbg.n += 16
            if "d_v" in dbg:
                @em("sync")
                def _(s, w=vS.n):
                    s.wait_ge(vS.h, w)
                    s.dma_start(dbg["d_v"], vtm[:, :]).then_inc(dDbg.h, 16)
                dDbg.n += 16
            if "d_h1T" in dbg:
                @em("sync")
                def _(s, w=vS.n):
                    s.wait_ge(vS.h, w)
                    s.dma_start(dbg["d_h1T"], h1Tb[:, :]).then_inc(dDbg.h, 16)
                dDbg.n += 16
            if "d_sml" in dbg:
                @em("sync")
                def _(s, w=vS.n, wc=cS.n):
                    s.wait_ge(vS.h, w)
                    s.wait_ge(cS.h, wc)
                    s.dma_start(dbg["d_sml"], sml[:, :]).then_inc(dDbg.h, 16)
                dDbg.n += 16
            if "d_h1b" in dbg:
                @em("sync")
                def _(s, w=cS.n):
                    s.wait_ge(cS.h, w)
                    s.dma_start(dbg["d_h1b"][:, 0:2048], h1b[0]).then_inc(dDbg.h, 16)
                    s.dma_start(dbg["d_h1b"][:, 2048:4096], h1b[1]).then_inc(dDbg.h, 16)
                dDbg.n += 32
            if "d_xst" in dbg:
                @em("sync")
                def _(s, w=cS.n):
                    s.wait_ge(cS.h, w)
                    s.dma_start(dbg["d_xst"][:, 0:2048], xst[0]).then_inc(dDbg.h, 16)
                    s.dma_start(dbg["d_xst"][:, 2048:4096], xst[1]).then_inc(dDbg.h, 16)
                dDbg.n += 32
            barrier()

        if "PBC" in phases:
            wprfree = [0, 0]
            xownfree = [0, 0]
            qpr_gate = [0]
            for og in range(2):
                h1free = [tS.n, tS.n]
                for tl in range(4):
                    ti = og * 4 + tl
                    sl = tl % 2

                    @em("sync")
                    def _(s, ti=ti, sl=sl, fv=xownfree[sl]):
                        s.wait_ge(cS.h, fv)
                        s.dma_start(resid[:, ti * 2048:(ti + 1) * 2048],
                                    x_own[ti * 128:(ti + 1) * 128, :]).then_inc(dX[sl].h, 16)
                    dX[sl].n += 16
                    h1v = emit_ln(resid[:, ti * 2048:(ti + 1) * 2048], h1b[sl], 120,
                                  (dX[sl], dX[sl].n), (tS, h1free[sl]), sl)
                    xownfree[sl] = h1v
                    trv, _ = emit_transpose_tile(h1b[sl], h1T3, tl, h1v)
                    h1free[sl] = trv
                h1T_done = vS.n

                for qs in range(4):
                    wsl = qs % 2

                    @em("gpsimd")
                    def _(g, qs=qs, wsl=wsl, wf=wprfree[wsl]):
                        g.wait_ge(tS.h, wf)
                        for c_ in range(16):
                            g.dma_start(wpr3[:, wsl, c_, :],
                                        wq_t[c_ * 128:(c_ + 1) * 128,
                                             qs * 512:(qs + 1) * 512]).then_inc(dWp[wsl].h, 16)
                    dWp[wsl].n += 256
                    wrdy = dWp[wsl].n
                    for hsub in range(4):
                        head = qs * 4 + hsub
                        pb = hsub % 2
                        L = [wpr3[:, wsl, c_, hsub * 128:(hsub + 1) * 128] for c_ in range(16)]
                        R = [h1T3[:, c_, :] for c_ in range(16)]

                        def ev(v, ps, head=head):
                            return v.tensor_copy(qT3[:, head, :], ps[:, :])
                        def ev2(v, ps, head=head):
                            return v.tensor_single_scalar(qT3[:, head, :], ps[:, :],
                                                          SCALE, ALU.mult)
                        emit_mm_evac(L, R, pb, ev2,
                                     waits=[(vS, h1T_done), (dWp[wsl], wrdy)])
                    wprfree[wsl] = tS.n
                qT_done = vS.n
                emit_prep(24)

                for head in range(NHEAD):
                    @em("scalar")
                    def _(c, head=head, w=qT_done, gate=tS.n):
                        c.wait_ge(vS.h, w)
                        c.wait_ge(tS.h, gate)
                        c.activation(sqb, qT3[:, head, :], AF.Square).then_inc(cS.h, 1)
                    cS.n += 1
                    sqv = cS.n

                    @em("tensor")
                    def _(t, sqv=sqv, pf=psfree[2]):
                        t.wait_ge(cS.h, sqv)
                        t.wait_ge(vS.h, pf)
                        t.matmul(psf[2][:, :], ones128[:, :], sqb, start=True,
                                 stop=True).then_inc(tS.h, 1)
                    tS.n += 1
                    mmv = tS.n

                    @em("vector")
                    def _(v, mmv=mmv, head=head):
                        v.wait_ge(tS.h, mmv)
                        v.reduce_max(sml[0:1, 80 + head:81 + head], psf[2][0:1, :],
                                     axis=AX.X).then_inc(vS.h, 1)
                    vS.n += 1
                    psfree[2] = vS.n

                @em("vector")
                def _(v, w=vS.n):
                    v.wait_ge(vS.h, w)
                    v.reduce_max(sml[0:1, 96:97], sml[0:1, 80:96], axis=AX.X).then_inc(vS.h, 1)
                vS.n += 1
                qmx = vS.n

                @em("scalar")
                def _(c, w=qmx):
                    c.wait_ge(vS.h, w)
                    c.activation(sml[0:1, 97:98], sml[0:1, 96:97], AF.Ln)
                    c.activation(sml[0:1, 98:99], sml[0:1, kmax_col:kmax_col + 1],
                                 AF.Ln).then_inc(cS.h, 1)
                cS.n += 1
                lns = cS.n

                @em("vector")
                def _(v, w=lns):
                    v.wait_ge(cS.h, w)
                    v.tensor_scalar(sml[0:1, 99:100], sml[0:1, 97:98],
                                    sml[0:1, 98:99], None, ALU.add,
                                    ALU.bypass).then_inc(vS.h, 1)
                vS.n += 1
                addv = vS.n

                @em("scalar")
                def _(c, w=addv):
                    c.wait_ge(vS.h, w)
                    c.activation(sml[0:1, 100:101], sml[0:1, 99:100], AF.Exp,
                                 scale=0.5).then_inc(cS.h, 1)
                cS.n += 1
                bv = cS.n

                @em("vector")
                def _(v, w=bv):
                    v.wait_ge(cS.h, w)
                    v.tensor_single_scalar(sml[0:1, 101:102], sml[0:1, 100:101], -1.02,
                                           ALU.mult).then_inc(vS.h, 1)
                vS.n += 1
                nbv = vS.n

                @em("sync")
                def _(s, og=og, qg=qpr_gate[0]):
                    s.wait_ge(tS.h, qg)
                    s.dma_start(qpr[0:1, :],
                                qpos_t[0:1, og * 512:og * 512 + 256]).then_inc(dQp.h, 16)
                dQp.n += 16
                qprv = dQp.n

                @em("tensor")
                def _(t, w=nbv, qprv=qprv, pf2=psfree[2], pf3=psfree[3]):
                    t.wait_ge(vS.h, max(w, pf2, pf3))
                    t.wait_ge(dQp.h, qprv)
                    t.matmul(psf[2][:, 0:1], onescolf[0:1, :], sml[0:1, 101:102],
                             start=True, stop=True)
                    t.matmul(psf[3][:, 0:256], onescolf[0:1, :], qpr[0:1, :],
                             start=True, stop=True).then_inc(tS.h, 1)
                tS.n += 1
                bc1 = tS.n

                @em("sync")
                def _(s, og=og, bc1=bc1):
                    s.wait_ge(tS.h, bc1)
                    s.dma_start(qpr[0:1, :],
                                qpos_t[0:1, og * 512 + 256:(og + 1) * 512]).then_inc(dQp.h, 16)
                dQp.n += 16
                qprv2 = dQp.n

                @em("tensor")
                def _(t, qprv2=qprv2):
                    t.wait_ge(dQp.h, qprv2)
                    t.matmul(psf[3][:, 256:512], onescolf[0:1, :], qpr[0:1, :],
                             start=True, stop=True).then_inc(tS.h, 1)
                tS.n += 1
                bcv = tS.n
                qpr_gate[0] = bcv

                @em("vector")
                def _(v, w=bcv):
                    v.wait_ge(tS.h, w)
                    v.tensor_copy(sml[:, 106:107], psf[2][:, 0:1])
                    v.tensor_copy(qposB, psf[3][:, :]).then_inc(vS.h, 1)
                vS.n += 1
                psfree[2] = vS.n
                psfree[3] = vS.n
                setup_done_v = vS.n

                if "d_qT" in dbg and og == 0:
                    @em("sync")
                    def _(s, w=qT_done):
                        s.wait_ge(vS.h, w)
                        s.dma_start(dbg["d_qT"], qTb[:, :]).then_inc(dDbg.h, 16)
                    dDbg.n += 16

                for kv in range(NKV):
                    for pr in range(2):
                        emit_prep(15)
                        h0 = kv * 4 + pr * 2
                        prev_mkv = 0
                        expfree = [0, 0]
                        expdone = [0]
                        scv_tab = {}
                        first_gate = max(setup_done_v, psfree[0], psfree[1],
                                         psfree[2], psfree[3], psfree[4], psfree[5])

                        def emit_scores(sb, kv=kv, h0=h0):
                            @em("tensor")
                            def _(t, sb=sb, kv=kv, h0=h0, scf=expdone[0],
                                  fe=(first_gate if sb == 0 else 0)):
                                if fe:
                                    t.wait_ge(vS.h, fe)
                                t.wait_ge(cS.h, scf)
                                t.matmul(psw[:, 0:512],
                                         kT[:, kv * S + sb * 128:kv * S + (sb + 1) * 128],
                                         qT3[:, h0, :], start=True, stop=True)
                                t.matmul(psw[:, 512:1024],
                                         kT[:, kv * S + sb * 128:kv * S + (sb + 1) * 128],
                                         qT3[:, h0 + 1, :], start=True,
                                         stop=True).then_inc(tS.h, 1)
                            tS.n += 1
                            scv_tab[sb] = tS.n

                        emit_scores(0)
                        for sb in range(32):
                            @em("scalar")
                            def _(c, sb=sb, scv=scv_tab[sb], ef=expfree[sb % 2]):
                                c.wait_ge(tS.h, scv)
                                if ef:
                                    c.wait_ge(tS.h, ef)
                                c.activation(expw[sb % 2][:, :], psw[:, :], AF.Exp,
                                             bias=sml[:, 106:107]).then_inc(cS.h, 1)
                            cS.n += 1
                            expv = cS.n
                            expdone[0] = expv

                            @em("vector")
                            def _(v, sb=sb, expv=expv, base=vS.n, qrdy=setup_done_v,
                                  pmk=prev_mkv):
                                if sb == 0:
                                    v.wait_ge(vS.h, qrdy)
                                else:
                                    v.wait_ge(vS.h, pmk)
                                v.tensor_single_scalar(maskb[0], qposB,
                                                       sposall[:, sb:sb + 1],
                                                       ALU.is_ge).then_inc(vS.h, 1)
                                v.wait_ge(vS.h, base + 1)
                                v.wait_ge(cS.h, expv)
                                v.tensor_mul(expb[sb % 2][0], expb[sb % 2][0],
                                             maskb[0])
                                v.tensor_mul(expb[sb % 2][1], expb[sb % 2][1],
                                             maskb[0]).then_inc(vS.h, 1)
                            vS.n += 2
                            mkv = vS.n
                            prev_mkv = mkv

                            if sb < 31:
                                emit_scores(sb + 1)

                            @em("tensor")
                            def _(t, kv=kv, sb=sb, mkv=mkv):
                                t.wait_ge(vS.h, mkv)
                                t.matmul(psf[2][:, :],
                                         vtm[:, sb * 512 + kv * 128:sb * 512 + (kv + 1) * 128],
                                         expb[sb % 2][0], start=(sb == 0), stop=(sb == 31))
                                t.matmul(psf[3][:, :],
                                         vtm[:, sb * 512 + kv * 128:sb * 512 + (kv + 1) * 128],
                                         expb[sb % 2][1], start=(sb == 0), stop=(sb == 31))
                                t.matmul(psf[4][:, :], ones128[:, :], expb[sb % 2][0],
                                         start=(sb == 0), stop=(sb == 31))
                                t.matmul(psf[5][:, :], ones128[:, :], expb[sb % 2][1],
                                         start=(sb == 0), stop=(sb == 31)).then_inc(tS.h, 1)
                            tS.n += 1
                            expfree[sb % 2] = tS.n
                        avsum_done = tS.n

                        @em("scalar")
                        def _(c, w=avsum_done, base=cS.n):
                            c.wait_ge(tS.h, w)
                            c.activation(recipb[0], psf[4][:, :], AF.Ln)
                            c.activation(recipb[1], psf[5][:, :], AF.Ln).then_inc(cS.h, 1)
                            c.wait_ge(cS.h, base + 1)
                            c.activation(recipb[0], recipb[0], AF.Exp, scale=-1.0)
                            c.activation(recipb[1], recipb[1], AF.Exp,
                                         scale=-1.0).then_inc(cS.h, 1)
                        cS.n += 2
                        rcv = cS.n

                        @em("vector")
                        def _(v, h0=h0, rcv=rcv):
                            v.wait_ge(cS.h, rcv)
                            v.tensor_tensor(aoT3[:, h0, :], psf[2][:, :], recipb[0], ALU.mult)
                            v.tensor_tensor(aoT3[:, h0 + 1, :], psf[3][:, :], recipb[1],
                                            ALU.mult).then_inc(vS.h, 1)
                        vS.n += 1
                        for b_ in (2, 3, 4, 5):
                            psfree[b_] = vS.n
                        setup_done_v = vS.n

                if "d_spos" in dbg and og == 0:
                    @em("sync")
                    def _(s, w=vS.n):
                        s.wait_ge(vS.h, w)
                        s.dma_start(dbg["d_spos"], sposall[:, :]).then_inc(dDbg.h, 16)
                        s.dma_start(dbg["d_qposB"], qposB).then_inc(dDbg.h, 16)
                        s.dma_start(dbg["d_mask"], maskb[0]).then_inc(dDbg.h, 16)
                    dDbg.n += 48
                if "d_aoT" in dbg and og == 0:
                    @em("sync")
                    def _(s, w=vS.n):
                        s.wait_ge(vS.h, w)
                        s.dma_start(dbg["d_aoT"], h1Tb[:, :]).then_inc(dDbg.h, 16)
                    dDbg.n += 16

                ao_done = vS.n
                for jq in range(4):
                    wsl = jq % 2

                    @em("gpsimd")
                    def _(g, jq=jq, wsl=wsl, wf=wprfree[wsl]):
                        g.wait_ge(tS.h, wf)
                        for c_ in range(16):
                            g.dma_start(wpr3[:, wsl, c_, :],
                                        wo_t[c_ * 128:(c_ + 1) * 128,
                                             jq * 512:(jq + 1) * 512]).then_inc(dWp[wsl].h, 16)
                    dWp[wsl].n += 256
                    wrdy = dWp[wsl].n
                    for qsub in range(4):
                        ti = og * 4 + qsub
                        pb = qsub % 2
                        L = [aoT3[:, c_, qsub * 128:(qsub + 1) * 128] for c_ in range(16)]
                        R = [wpr3[:, wsl, c_, :] for c_ in range(16)]

                        def ev(v, ps, ti=ti, jq=jq):
                            dst = resid[:, ti * 2048 + jq * 512:ti * 2048 + (jq + 1) * 512]
                            return v.tensor_tensor(dst, ps[:, :], dst, ALU.add)
                        emit_mm_evac(L, R, pb, ev, waits=[(vS, ao_done), (dWp[wsl], wrdy)])
                    wprfree[wsl] = tS.n
            emit_prep(NPREP)
            if "d_resid" in dbg:
                @em("sync")
                def _(s, w=vS.n):
                    s.wait_ge(vS.h, w)
                    s.dma_start(dbg["d_resid"], resid[:, :]).then_inc(dDbg.h, 16)
                dDbg.n += 16
            barrier()

        if "PE" in phases:
            prev_done = (bar, bar.n)
            for ti in range(8):
                par = ti % 2
                h2v = emit_ln(resid[:, ti * 2048:(ti + 1) * 2048], h2s, 120,
                              prev_done, None, par)
                h3v = emit_ln(h2s, h3s, 121, (cS, h2v), None, par)

                @em("vector")
                def _(v, w=h3v, b0=par * 16):
                    v.wait_ge(cS.h, w)
                    v.reduce_max(sml[:, b0 + 8:b0 + 9], h3s, axis=AX.X,
                                 apply_absolute_value=True).then_inc(vS.h, 1)
                vS.n += 1
                amv = vS.n

                @em("scalar")
                def _(c, w=amv, b0=par * 16, base=cS.n):
                    c.wait_ge(vS.h, w)
                    c.activation(sml[:, b0 + 9:b0 + 10], sml[:, b0 + 8:b0 + 9],
                                 AF.Ln).then_inc(cS.h, 1)
                    c.wait_ge(cS.h, base + 1)
                    c.activation(sml[:, b0 + 10:b0 + 11], sml[:, b0 + 9:b0 + 10], AF.Exp,
                                 scale=-1.0).then_inc(cS.h, 1)
                cS.n += 2
                invv = cS.n

                @em("vector")
                def _(v, w=invv, b0=par * 16):
                    v.wait_ge(cS.h, w)
                    v.tensor_single_scalar(sml[:, b0 + 10:b0 + 11],
                                           sml[:, b0 + 10:b0 + 11], 127.0,
                                           ALU.mult).then_inc(vS.h, 1)
                vS.n += 1
                sv_ = vS.n

                @em("scalar")
                def _(c, w=sv_, b0=par * 16, base=cS.n):
                    c.wait_ge(vS.h, w)
                    c.activation(h2s, h3s, AF.Identity,
                                 scale=sml[:, b0 + 10:b0 + 11])
                    c.activation(sml[:, b0 + 9:b0 + 10], sml[:, b0 + 10:b0 + 11],
                                 AF.Ln).then_inc(cS.h, 1)
                    c.wait_ge(cS.h, base + 1)
                    c.activation(sml[:, b0 + 11:b0 + 12], sml[:, b0 + 9:b0 + 10], AF.Exp,
                                 scale=-1.0).then_inc(cS.h, 1)
                cS.n += 2
                yv = cS.n

                @em("vector")
                def _(v, w=yv, base=vS.n):
                    v.wait_ge(cS.h, w)
                    v.tensor_copy(i16s, h2s).then_inc(vS.h, 1)
                    v.wait_ge(vS.h, base + 1)
                    v.tensor_single_scalar(ifs, i16s, 1.0, ALU.mult).then_inc(vS.h, 1)
                    v.wait_ge(vS.h, base + 2)
                    v.tensor_tensor(h3s, h2s, ifs, ALU.subtract).then_inc(vS.h, 1)
                    v.wait_ge(vS.h, base + 3)
                    v.scalar_tensor_tensor(h2s, h3s, 0.5, ifs, ALU.is_ge,
                                           ALU.add).then_inc(vS.h, 1)
                    v.wait_ge(vS.h, base + 4)
                    v.tensor_scalar(h3s, h3s, -0.5, -1.0, ALU.is_lt,
                                    ALU.mult).then_inc(vS.h, 1)
                    v.wait_ge(vS.h, base + 5)
                    v.tensor_tensor(h2s, h2s, h3s, ALU.add).then_inc(vS.h, 1)
                vS.n += 6
                rdv = vS.n

                @em("scalar")
                def _(c, w=rdv, b0=par * 16):
                    c.wait_ge(vS.h, w)
                    c.activation(xqb, h2s, AF.Identity,
                                 scale=sml[:, b0 + 11:b0 + 12]).then_inc(cS.h, 1)
                cS.n += 1
                xqv = cS.n
                trv, _ = emit_transpose_tile(xqb, xqT3, ti, xqv)
                prev_done = (vS, vS.n)

            @em("gpsimd")
            def _(g, w=prev_done[1]):
                g.wait_ge(vS.h, w)
                for ti in range(8):
                    g.dma_start(scrr[ti * 128:(ti + 1) * 128, :],
                                resid[:, ti * 2048:(ti + 1) * 2048]).then_inc(dEv.h, 16)
            dEv.n += 128
            if "d_xqT" in dbg:
                @em("sync")
                def _(s, w=prev_done[1]):
                    s.wait_ge(vS.h, w)
                    s.dma_start(dbg["d_xqT"], xqT3[:, :, :]).then_inc(dDbg.h, 16)
                dDbg.n += 16
            barrier()

            @em("scalar")
            def _(c):
                c.add_instruction(mybir.InstLoadActFuncSet(
                    name=nc.get_next_instruction_name(), act_func_set_id=18, ins=[], outs=[]))

            for pss in range(2):
                wfree = 0
                gsfree = [0, 0]
                for pr2 in range(8):
                    par2 = pr2 % 2

                    @em("sync")
                    def _(s, pr2=pr2, par2=par2, wf=wfree):
                        s.wait_ge(tS.h, wf)
                        for c_ in range(16):
                            s.dma_start_transpose(
                                wg3[:, c_, :],
                                scrg[pr2 * 1024:(pr2 + 1) * 1024,
                                     c_ * 128:(c_ + 1) * 128]).then_inc(dTR[par2].h, 16)
                        for c_ in range(16):
                            s.dma_start_transpose(
                                wu3[:, c_, :],
                                scru[pr2 * 1024:(pr2 + 1) * 1024,
                                     c_ * 128:(c_ + 1) * 128]).then_inc(dTR[par2].h, 16)
                    dTR[par2].n += 512
                    wrdy = dTR[par2].n
                    for hsub in range(8):
                        pbg = hsub % 2
                        pbu = 2 + hsub % 2

                        @em("tensor")
                        def _(t, pss=pss, par2=par2, hsub=hsub, pbg=pbg, pbu=pbu, wrdy=wrdy,
                              pfg=psfree[pbg], pfu=psfree[pbu]):
                            t.wait_ge(dTR[par2].h, wrdy)
                            t.wait_ge(vS.h, max(pfg, pfu))
                            for c_ in range(16):
                                ins = t.matmul(psf[pbg][:, :],
                                               wg3[:, c_, hsub * 128:(hsub + 1) * 128],
                                               xqT3[:, c_, pss * 512:(pss + 1) * 512],
                                               start=(c_ == 0), stop=(c_ == 15))
                            for c_ in range(16):
                                ins = t.matmul(psf[pbu][:, :],
                                               wu3[:, c_, hsub * 128:(hsub + 1) * 128],
                                               xqT3[:, c_, pss * 512:(pss + 1) * 512],
                                               start=(c_ == 0), stop=(c_ == 15))
                            ins.then_inc(tS.h, 1)
                        tS.n += 1
                        guv = tS.n

                        @em("scalar")
                        def _(c, pbg=pbg, guv=guv, gsl=hsub % 2, gf=gsfree[hsub % 2]):
                            c.wait_ge(tS.h, guv)
                            c.wait_ge(vS.h, gf)
                            c.activation(gsb[gsl], psf[pbg][:, :], AF.Silu).then_inc(cS.h, 1)
                        cS.n += 1
                        slv = cS.n

                        @em("vector")
                        def _(v, pss=pss, pr2=pr2, hsub=hsub, pbu=pbu, slv=slv, gsl=hsub % 2):
                            v.wait_ge(cS.h, slv)
                            slot = pr2 * 8 + hsub
                            v.tensor_tensor(pT[:, slot * 512:(slot + 1) * 512],
                                            gsb[gsl], psf[pbu][:, :],
                                            ALU.mult).then_inc(vS.h, 1)
                        vS.n += 1
                        psfree[pbg] = vS.n
                        psfree[pbu] = vS.n
                        gsfree[hsub % 2] = vS.n
                    wfree = tS.n
                pT_done = vS.n
                if "d_pT" in dbg and pss == 0:
                    @em("sync")
                    def _(s, w=pT_done):
                        s.wait_ge(vS.h, w)
                        s.dma_start(dbg["d_pT"], pT[:, :]).then_inc(dDbg.h, 16)
                    dDbg.n += 16

                outfree = [0, 0]
                @em("sync")
                def _(s, pss=pss):
                    for tl in range(4):
                        ti = pss * 4 + tl
                        s.dma_start(resst[:, tl * 2048:(tl + 1) * 2048],
                                    scrr[ti * 128:(ti + 1) * 128, :]).then_inc(dRes.h, 16)
                dRes.n += 64
                res_rdy = dRes.n

                wd_free = [tS.n, tS.n]
                for jq in range(4):
                    @em("sync")
                    def _(s, jq=jq, wf=wd_free[0]):
                        s.wait_ge(tS.h, wf)
                        for hh in range(32):
                            s.dma_start_transpose(
                                wd3k[:, hh, :],
                                scrd[jq * 512:(jq + 1) * 512,
                                     hh * 128:(hh + 1) * 128]).then_inc(dTR[0].h, 16)
                    dTR[0].n += 512

                    @em("sync")
                    def _(s, jq=jq, wf=wd_free[1]):
                        s.wait_ge(tS.h, wf)
                        for hh in range(32):
                            s.dma_start_transpose(
                                wd3v[:, hh, :],
                                scrd[jq * 512:(jq + 1) * 512,
                                     (32 + hh) * 128:(33 + hh) * 128]).then_inc(dTR[1].h, 16)
                    dTR[1].n += 512

                    for tl in range(4):
                        pb = tl % 2

                        @em("tensor")
                        def _(t, jq=jq, tl=tl, pb=pb, wr0=dTR[0].n, wr1=dTR[1].n,
                              pf=psfree[pb], pd=pT_done):
                            t.wait_ge(dTR[0].h, wr0)
                            t.wait_ge(dTR[1].h, wr1)
                            t.wait_ge(vS.h, max(pf, pd))
                            for hh in range(64):
                                arena = wd3k if hh < 32 else wd3v
                                ins = t.matmul(psf[pb][:, :],
                                               pT[:, hh * 512 + tl * 128:hh * 512 + (tl + 1) * 128],
                                               arena[:, hh % 32, :],
                                               start=(hh == 0), stop=(hh == 63))
                            ins.then_inc(tS.h, 1)
                        tS.n += 1
                        mmv = tS.n

                        @em("vector")
                        def _(v, jq=jq, tl=tl, pb=pb, mmv=mmv, rr=res_rdy, osl=tl % 2,
                              of=outfree[tl % 2]):
                            v.wait_ge(tS.h, mmv)
                            v.wait_ge(dRes.h, rr)
                            v.wait_ge(dOutS[osl].h, of)
                            v.tensor_tensor(outst[osl], psf[pb][:, :],
                                            resst[:, tl * 2048 + jq * 512:
                                                  tl * 2048 + (jq + 1) * 512],
                                            ALU.add).then_inc(vS.h, 1)
                        vS.n += 1
                        psfree[pb] = vS.n
                        ov = vS.n

                        @em("gpsimd")
                        def _(g, pss=pss, jq=jq, tl=tl, ov=ov, osl=tl % 2):
                            g.wait_ge(vS.h, ov)
                            ti = pss * 4 + tl
                            g.dma_start(out_t[ti * 128:(ti + 1) * 128,
                                              jq * 512:(jq + 1) * 512],
                                        outst[osl]).then_inc(dOutS[osl].h, 16)
                        dOutS[tl % 2].n += 16
                        outfree[tl % 2] = dOutS[tl % 2].n
                    wd_free = [tS.n, tS.n]
                barrier()

        barrier()

        with nc.Block() as block:
            block.tensor(lambda t: [f(t) for f in ES["tensor"]])
            block.vector(lambda v: [f(v) for f in ES["vector"]])
            block.scalar(lambda c: [f(c) for f in ES["scalar"]])
            block.gpsimd(lambda g: [f(g) for f in ES["gpsimd"]])
            block.sync(lambda s: [f(s) for f in ES["sync"]])
    return nc


_CACHED = {}


def _get_nc(key=("PA", "PBC", "PE"), debug=()):
    k = (tuple(key), tuple(debug))
    if k not in _CACHED:
        _CACHED[k] = build(key, debug)
    return _CACHED[k]


def kernel(**inputs):
    x = np.ascontiguousarray(inputs["L0_attn_x"])
    wq = np.ascontiguousarray(inputs["L0_attn_Wq"])
    wk = np.ascontiguousarray(inputs["L0_attn_Wk"])
    wv = np.ascontiguousarray(inputs["L0_attn_Wv"])
    wo = np.ascontiguousarray(inputs["L0_attn_Wo"])
    wg = np.ascontiguousarray(inputs["L0_ffn_W_g"])
    wu = np.ascontiguousarray(inputs["L0_ffn_W_u"])
    wd = np.ascontiguousarray(inputs["L0_ffn_W_d"])

    nc = _get_nc()
    in_maps = []
    for c in range(8):
        b, j = c // 4, c % 4
        own = np.concatenate([x[b, j * 512:(j + 1) * 512],
                              x[b, (7 - j) * 512:(8 - j) * 512]], axis=0)
        qpos = np.concatenate([np.arange(j * 512, (j + 1) * 512),
                               np.arange((7 - j) * 512, (8 - j) * 512)]).astype(np.float32)
        in_maps.append({
            "x_full": x[b], "x_own": own, "qpos": qpos[None, :],
            "wq": wq, "wk": wk, "wv": wv, "wo": wo, "wg": wg, "wu": wu, "wd": wd,
        })
    res = run_bass_kernel_spmd(nc, in_maps, core_ids=list(range(8)))
    out = np.empty((B, S, D), dtype=np.float32)
    for c in range(8):
        b, j = c // 4, c % 4
        o = res.results[c]["out"]
        out[b, j * 512:(j + 1) * 512] = o[0:512]
        out[b, (7 - j) * 512:(8 - j) * 512] = o[512:1024]
    return out


# version 10
# speedup vs baseline: 1.0672x; 1.0672x over previous
import numpy as np
import ml_dtypes
import concourse.bass as bass
import concourse.mybir as mybir
from concourse.bass_utils import run_bass_kernel_spmd
from contextlib import ExitStack

F32 = mybir.dt.float32
BF16 = mybir.dt.bfloat16
I16 = mybir.dt.int16
AF = mybir.ActivationFunctionType
ALU = mybir.AluOpType
AX = mybir.AxisListType

B, S, D, H = 2, 4096, 2048, 8192
NKV, DH, NHEAD = 4, 128, 16
SCALE = float(DH) ** -0.5
TOK = 1024
SG = S // 512
LN127 = float(np.log(127.0))
EPS1, EPS2 = 1e-5, 1e-8


class Ctr:
    def __init__(self, st, nc, name):
        self.h = st.enter_context(nc.semaphore(name))
        self.n = 0


def build(phases=("PA", "PBC", "PE"), debug=()):
    nc = bass.Bass("TRN2", target_bir_lowering=False)
    ES = {e: [] for e in ("tensor", "vector", "scalar", "gpsimd", "sync")}

    def em(eng):
        def reg(f):
            ES[eng].append(f)
            return f
        return reg

    x_full = nc.dram_tensor("x_full", [S, D], F32, kind="ExternalInput").ap()
    x_own = nc.dram_tensor("x_own", [TOK, D], F32, kind="ExternalInput").ap()
    qpos_t = nc.dram_tensor("qpos", [1, TOK], F32, kind="ExternalInput").ap()
    wq_t = nc.dram_tensor("wq", [D, D], F32, kind="ExternalInput").ap()
    wk_t = nc.dram_tensor("wk", [D, 512], F32, kind="ExternalInput").ap()
    wv_t = nc.dram_tensor("wv", [D, 512], F32, kind="ExternalInput").ap()
    wo_t = nc.dram_tensor("wo", [D, D], F32, kind="ExternalInput").ap()
    wg_t = nc.dram_tensor("wg", [H, D], F32, kind="ExternalInput").ap()
    wu_t = nc.dram_tensor("wu", [H, D], F32, kind="ExternalInput").ap()
    wd_t = nc.dram_tensor("wd", [D, H], F32, kind="ExternalInput").ap()
    out_t = nc.dram_tensor("out", [TOK, D], F32, kind="ExternalOutput").ap()
    scrg = nc.dram_tensor("scrg", [H, D], BF16, kind="Internal").ap()
    scru = nc.dram_tensor("scru", [H, D], BF16, kind="Internal").ap()
    scrd = nc.dram_tensor("scrd", [D, H], BF16, kind="Internal").ap()
    scrr = nc.dram_tensor("scrr", [TOK, D], F32, kind="Internal").ap()

    dbg = {}
    dbg_specs = {
        "d_kT": ([128, NKV * S], BF16), "d_v": ([128, 32 * 512], BF16),
        "d_qT": ([128, NHEAD * 512], BF16), "d_resid": ([128, 8 * D], F32),
        "d_xqT": ([128, 16 * TOK], BF16), "d_pT": ([128, 64 * 512], BF16),
        "d_aoT": ([128, NHEAD * 512], BF16), "d_h1T": ([128, 16 * 512], BF16),
        "d_sml": ([128, 128], F32), "d_h1b": ([128, 2 * 2048], BF16),
        "d_xst": ([128, 2 * 2048], F32), "d_spos": ([128, 32], F32),
        "d_qposB": ([128, 512], F32), "d_mask": ([128, 512], BF16),
    }
    for name in debug:
        shape, dt = dbg_specs[name]
        dbg[name] = nc.dram_tensor(name, shape, dt, kind="ExternalOutput").ap()

    st = ExitStack()
    with st:
        ones128 = st.enter_context(nc.sbuf_tensor("ones128", [128, 128], BF16))
        identT = st.enter_context(nc.sbuf_tensor("identT", [128, 128], BF16))
        onescolf = st.enter_context(nc.sbuf_tensor("onescolf", [1, 128], F32))
        sposall = st.enter_context(nc.sbuf_tensor("sposall", [128, 32], F32))
        sml = st.enter_context(nc.sbuf_tensor("smalls", [128, 128], F32))
        qpr = st.enter_context(nc.sbuf_tensor("qpr", [1, 256], F32))

        MBASE = (nc.sbuf_top - 208896) & ~31
        assert MBASE >= nc.sbuf_base + 256, f"auto region too small: {nc.sbuf_base} vs {MBASE}"

        def at(name, shape, dtype, off):
            return nc.alloc_sbuf_tensor_at(name, shape, dtype, offset=MBASE + off)

        RA, RBK, RBV, RC, RD = 0, 65536, 98304, 131072, 163840
        resid = at("resid", [128, 16384], F32, RA)
        pT = at("pT", [128, 32768], BF16, RA)
        kT = at("kT", [128, 16384], BF16, RBK)
        wg3 = at("wg3", [128, 2, 16, 512], BF16, RBK)
        wd3k = at("wd3k", [128, 32, 512], BF16, RBK)
        vtm = at("vtm", [128, 16384], BF16, RBV)
        wu3 = at("wu3", [128, 2, 16, 512], BF16, RBV)
        wd3v = at("wd3v", [128, 32, 512], BF16, RBV)
        wkv = at("wkv", [128, 16384], BF16, RC)
        wpr3 = at("wpr3", [128, 2, 16, 512], BF16, RC)
        xqT3 = at("xqT3", [128, 16, 1024], BF16, RC)
        xqTf = at("xqTf", [128, 16384], BF16, RC)
        xst = [at("xst0", [128, 2048], F32, RD + 0)[:, :], at("xst1", [128, 2048], F32, RD + 8192)[:, :]]
        h1b = [at("h1b0", [128, 2048], BF16, RD + 16384)[:, :],
               at("h1b1", [128, 2048], BF16, RD + 20480)[:, :]]
        h1T3 = at("h1T3", [128, 16, 512], BF16, RD + 24576)
        h1Tb = at("h1Tb", [128, 8192], BF16, RD + 24576)
        pin = [at("pin0", [128, 1024], BF16, RD + 40960)[:, :],
               at("pin1", [128, 1024], BF16, RD + 43008)[:, :]]
        qT3 = at("qT3", [128, 16, 512], BF16, RD + 0)
        qTb = at("qTb", [128, 8192], BF16, RD + 0)
        expw = [at("expw0", [128, 1024], BF16, RD + 16384),
                at("expw1", [128, 1024], BF16, RD + 18432)]
        expb = [[expw[p][:, h * 512:(h + 1) * 512] for h in range(2)] for p in range(2)]
        maskb = [at("maskb0", [128, 512], BF16, RD + 20480)[:, :]]
        sqb = at("sqb", [128, 512], BF16, RD + 21504)[:, :]
        recipb = [at("recip0", [128, 512], F32, RD + 16384)[:, :],
                  at("recip1", [128, 512], F32, RD + 18432)[:, :]]
        qposB = at("qposB", [128, 512], F32, RD + 22528)[:, :]
        aoT3 = h1T3
        h2s = at("h2s", [128, 2048], F32, RD + 0)[:, :]
        h3s = at("h3s", [128, 2048], F32, RD + 8192)[:, :]
        i16s = at("i16s", [128, 2048], I16, RD + 16384)[:, :]
        ifs = at("ifs", [128, 2048], F32, RD + 20480)[:, :]
        xqb = at("xqb", [128, 2048], BF16, RD + 28672)[:, :]
        resst = at("resst", [128, 8192], F32, RD + 0)
        outst = [at("outst0", [128, 512], F32, RD + 32768)[:, :],
                 at("outst1", [128, 512], F32, RD + 34816)[:, :]]
        gsb = [at("gs0", [128, 512], F32, RD + 36864)[:, :],
               at("gs1", [128, 512], F32, RD + 38912)[:, :]]

        psw = st.enter_context(nc.psum_tensor("psw", [128, 1024], F32))
        ps_tr = [st.enter_context(nc.psum_tensor(f"pstr{i}", [128, 512], BF16)) for i in range(2)]
        psf = [psw[:, 0:512], psw[:, 512:1024]] + \
            [st.enter_context(nc.psum_tensor(f"psf{i}", [128, 512], F32)) for i in range(4)]

        dX = [Ctr(st, nc, "dX0"), Ctr(st, nc, "dX1")]
        dKV = Ctr(st, nc, "dKV")
        dPin = [Ctr(st, nc, "dPin0"), Ctr(st, nc, "dPin1")]
        dQp = Ctr(st, nc, "dQp")
        dWp = [Ctr(st, nc, "dWp0"), Ctr(st, nc, "dWp1")]
        dSC = [Ctr(st, nc, "dSC0"), Ctr(st, nc, "dSC1")]
        dTR = [Ctr(st, nc, "dTR0"), Ctr(st, nc, "dTR1")]
        dRes = Ctr(st, nc, "dRes")
        dEv = Ctr(st, nc, "dEv")
        dOutS = [Ctr(st, nc, "dOut0"), Ctr(st, nc, "dOut1")]
        dDbg = Ctr(st, nc, "dDbg")
        tS = Ctr(st, nc, "tS")
        vS = Ctr(st, nc, "vS")
        cS = Ctr(st, nc, "cS")
        gS = Ctr(st, nc, "gS")
        bar = Ctr(st, nc, "bar")
        nbar = [0]

        def barrier():
            nbar[0] += 1
            tgt = 5 * nbar[0]
            gsems = [(c.h, c.n) for c in [dKV, dPin[0], dPin[1], dWp[0], dWp[1], dEv,
                                          dOutS[0], dOutS[1]]]
            ssems = [(c.h, c.n) for c in [dX[0], dX[1], dQp, dTR[0], dTR[1], dRes, dDbg]]
            csems = [(dSC[0].h, dSC[0].n), (dSC[1].h, dSC[1].n)]

            @em("tensor")
            def _(t, tgt=tgt):
                t.drain()
                t.sem_inc(bar.h, 1)
                t.wait_ge(bar.h, tgt)

            @em("vector")
            def _(v, tgt=tgt):
                v.drain()
                v.sem_inc(bar.h, 1)
                v.wait_ge(bar.h, tgt)

            @em("scalar")
            def _(c, tgt=tgt, ws=tuple(csems)):
                c.drain()
                for h, n in ws:
                    c.wait_ge(h, n)
                c.sem_inc(bar.h, 1)
                c.wait_ge(bar.h, tgt)

            @em("gpsimd")
            def _(g, tgt=tgt, ws=tuple(gsems)):
                for h, n in ws:
                    g.wait_ge(h, n)
                g.sem_inc(bar.h, 1)
                g.wait_ge(bar.h, tgt)

            @em("sync")
            def _(s, tgt=tgt, ws=tuple(ssems)):
                for h, n in ws:
                    s.wait_ge(h, n)
                s.sem_inc(bar.h, 1)
                s.wait_ge(bar.h, tgt)
            bar.n = tgt

        @em("gpsimd")
        def _(g):
            g.memset(ones128[:, :], 1.0).then_inc(gS.h, 1)
            g.memset(identT[:, :], 1.0).then_inc(gS.h, 1)
            g.memset(onescolf[0:1, :], 1.0).then_inc(gS.h, 1)
            g.memset(sml[:, 120:121], EPS1).then_inc(gS.h, 1)
            g.memset(sml[:, 121:122], EPS2).then_inc(gS.h, 1)
            g.memset(sml[:, 122:123], LN127).then_inc(gS.h, 1)
            g.memset(sml[:, 123:124], -LN127).then_inc(gS.h, 1)
            g.iota(sposall[:, :], [[128, 32]], channel_multiplier=1,
                   allow_small_or_imprecise_dtypes=True).then_inc(gS.h, 1)
            g.wait_ge(gS.h, 8)
            g.affine_select(out=identT[:, :], in_=identT[:, :], compare_op=ALU.is_equal,
                            fill=0.0, base=0, pattern=[[-1, 128]],
                            channel_multiplier=1).then_inc(gS.h, 1)
            g.wait_ge(gS.h, 9)
        gS.n += 9

        @em("scalar")
        def _(c):
            c.add_instruction(mybir.InstLoadActFuncSet(
                name=nc.get_next_instruction_name(), act_func_set_id=6, ins=[], outs=[]))
        barrier()

        prep_list = []
        for blk in range(H // 128):
            for half in range(2):
                prep_list.append((wg_t, scrg, blk, half))
        for blk in range(H // 128):
            for half in range(2):
                prep_list.append((wu_t, scru, blk, half))
        for blk in range(D // 128):
            for part in range(8):
                prep_list.append((wd_t, scrd, blk, part))
        NPREP = len(prep_list)
        pst = {"i": 0, "free": [0, 0]}

        def emit_prep(n):
            cnt = 0
            while cnt < n and pst["i"] < NPREP:
                i = pst["i"]
                pst["i"] += 1
                cnt += 1
                src_t, dst_t, blk, part = prep_list[i]
                sl = i % 2
                tile = pin[sl]
                freev = pst["free"][sl]

                @em("gpsimd")
                def _(g, src_t=src_t, blk=blk, part=part, tile=tile, freev=freev, sl=sl):
                    g.wait_ge(dSC[sl].h, freev)
                    g.dma_start(tile, src_t[blk * 128:(blk + 1) * 128,
                                            part * 1024:(part + 1) * 1024]).then_inc(dPin[sl].h, 16)
                dPin[sl].n += 16
                inv = dPin[sl].n

                @em("scalar")
                def _(c, tile=tile, inv=inv, sl=sl):
                    c.wait_ge(dPin[sl].h, inv)
                    c.activation(tile, tile, AF.Sign).then_inc(cS.h, 1)
                cS.n += 1
                sgv = cS.n
                dSC[sl].n += 16

                @em("scalar")
                def _(c, dst_t=dst_t, blk=blk, part=part, tile=tile, sgv=sgv, sl=sl):
                    c.wait_ge(cS.h, sgv)
                    c.dma_start(dst_t[blk * 128:(blk + 1) * 128,
                                      part * 1024:(part + 1) * 1024], tile).then_inc(dSC[sl].h, 16)
                pst["free"][sl] = dSC[sl].n

        def emit_ln(src_ap, dst_ap, eps, rdy, gate, par):
            b0 = par * 16

            @em("scalar")
            def _(c, src_ap=src_ap, dst_ap=dst_ap, rdy=rdy, gate=gate, b0=b0, base0=cS.n):
                if rdy is not None:
                    c.wait_ge(rdy[0].h, rdy[1])
                if gate is not None:
                    c.wait_ge(gate[0].h, gate[1])
                c.activation(dst_ap, src_ap, AF.Square,
                             accum_out=sml[:, b0 + 1:b0 + 2]).then_inc(cS.h, 1)
                c.wait_ge(cS.h, base0 + 1)
                c.activation(dst_ap, src_ap, AF.Copy,
                             accum_out=sml[:, b0 + 0:b0 + 1]).then_inc(cS.h, 1)
            cS.n += 2
            st1 = cS.n

            @em("vector")
            def _(v, st1=st1, b0=b0, base=vS.n):
                v.wait_ge(cS.h, st1)
                v.tensor_single_scalar(sml[:, b0 + 2:b0 + 3], sml[:, b0 + 0:b0 + 1],
                                       1.0 / D, ALU.mult)
                v.tensor_single_scalar(sml[:, b0 + 4:b0 + 5], sml[:, b0 + 1:b0 + 2],
                                       1.0 / D, ALU.mult).then_inc(vS.h, 1)
                v.wait_ge(vS.h, base + 1)
                v.tensor_scalar(sml[:, b0 + 3:b0 + 4], sml[:, b0 + 2:b0 + 3],
                                sml[:, b0 + 2:b0 + 3], -1.0, ALU.mult,
                                ALU.mult).then_inc(vS.h, 1)
                v.wait_ge(vS.h, base + 2)
                v.tensor_scalar(sml[:, b0 + 3:b0 + 4], sml[:, b0 + 4:b0 + 5],
                                sml[:, b0 + 3:b0 + 4], None, ALU.add,
                                ALU.bypass).then_inc(vS.h, 1)
            vS.n += 3
            stv = vS.n

            @em("scalar")
            def _(c, stv=stv, b0=b0, eps=eps, base=cS.n):
                c.wait_ge(vS.h, stv)
                c.activation(sml[:, b0 + 4:b0 + 5], sml[:, b0 + 3:b0 + 4], AF.Ln,
                             bias=sml[:, eps:eps + 1]).then_inc(cS.h, 1)
                c.wait_ge(cS.h, base + 1)
                c.activation(sml[:, b0 + 5:b0 + 6], sml[:, b0 + 4:b0 + 5], AF.Exp,
                             scale=-0.5).then_inc(cS.h, 1)
            cS.n += 2
            rsv = cS.n

            @em("vector")
            def _(v, rsv=rsv, b0=b0):
                v.wait_ge(cS.h, rsv)
                v.tensor_scalar(sml[:, b0 + 6:b0 + 7], sml[:, b0 + 2:b0 + 3],
                                sml[:, b0 + 5:b0 + 6], -1.0, ALU.mult,
                                ALU.mult).then_inc(vS.h, 1)
            vS.n += 1
            nmv = vS.n

            @em("scalar")
            def _(c, src_ap=src_ap, dst_ap=dst_ap, nmv=nmv, b0=b0):
                c.wait_ge(vS.h, nmv)
                c.activation(dst_ap, src_ap, AF.Identity, scale=sml[:, b0 + 5:b0 + 6],
                             bias=sml[:, b0 + 6:b0 + 7]).then_inc(cS.h, 1)
            cS.n += 1
            return cS.n

        trps_free = [0, 0]

        def emit_transpose_tile(src_bf_ap, dst3, tl, h1v, extra_tgate=0):
            for c4 in range(4):
                pfv = trps_free[c4 % 2]

                @em("tensor")
                def _(t, src=src_bf_ap, c4=c4, h1v=h1v, pfv=pfv, eg=extra_tgate):
                    t.wait_ge(cS.h, h1v)
                    t.wait_ge(vS.h, pfv)
                    if eg:
                        t.wait_ge(tS.h, eg)
                    for cc in range(4):
                        c_ = c4 * 4 + cc
                        ins = t.transpose(ps_tr[c4 % 2][:, cc * 128:(cc + 1) * 128],
                                          src[:, c_ * 128:(c_ + 1) * 128], identT[:, :])
                    ins.then_inc(tS.h, 1)
                tS.n += 1
                trv = tS.n

                @em("vector")
                def _(v, c4=c4, tl=tl, trv=trv, dst3=dst3):
                    v.wait_ge(tS.h, trv)
                    v.tensor_copy(dst3[:, c4 * 4:(c4 + 1) * 4, tl * 128:(tl + 1) * 128],
                                  ps_tr[c4 % 2][:, :]).then_inc(vS.h, 1)
                vS.n += 1
                trps_free[c4 % 2] = vS.n
            return tS.n, vS.n

        psfree = [0] * 6

        def emit_mm_evac(lhs_slices, rhs_slices, pb, evac_fn, waits=()):
            @em("tensor")
            def _(t, L=lhs_slices, R=rhs_slices, pb=pb, pf=psfree[pb], waits=tuple(waits)):
                for ctr, val in waits:
                    t.wait_ge(ctr.h, val)
                t.wait_ge(vS.h, pf)
                n = len(L)
                for i_ in range(n):
                    ins = t.matmul(psf[pb][:, :], L[i_], R[i_],
                                   start=(i_ == 0), stop=(i_ == n - 1))
                ins.then_inc(tS.h, 1)
            tS.n += 1
            mmv = tS.n

            @em("vector")
            def _(v, pb=pb, mmv=mmv, evac_fn=evac_fn):
                v.wait_ge(tS.h, mmv)
                evac_fn(v, psf[pb]).then_inc(vS.h, 1)
            vS.n += 1
            psfree[pb] = vS.n
            return vS.n

        kmax_col = 72
        if "PA" in phases:
            @em("gpsimd")
            def _(g):
                for c_ in range(16):
                    g.dma_start(wkv[:, c_ * 1024:c_ * 1024 + 512],
                                wk_t[c_ * 128:(c_ + 1) * 128, :]).then_inc(dKV.h, 16)
                    g.dma_start(wkv[:, c_ * 1024 + 512:(c_ + 1) * 1024],
                                wv_t[c_ * 128:(c_ + 1) * 128, :]).then_inc(dKV.h, 16)
            dKV.n += 512
            wkv_rdy = dKV.n

            xfree = [0, 0]
            h1free = [0, 0]
            kv_gate = 0
            for grp in range(SG):
                emit_prep(12)
                for tl in range(4):
                    ti = grp * 4 + tl
                    sl = ti % 2

                    @em("sync")
                    def _(s, ti=ti, sl=sl, fv=xfree[sl]):
                        s.wait_ge(cS.h, fv)
                        s.dma_start(xst[sl],
                                    x_full[ti * 128:(ti + 1) * 128, :]).then_inc(dX[sl].h, 16)
                    dX[sl].n += 16
                    h1v = emit_ln(xst[sl], h1b[sl], 120, (dX[sl], dX[sl].n),
                                  (tS, h1free[sl]), sl)
                    xfree[sl] = h1v
                    trv, _ = emit_transpose_tile(h1b[sl], h1T3, tl, h1v,
                                                 extra_tgate=(kv_gate if tl == 0 else 0))
                    h1free[sl] = trv
                kvw = vS.n
                for k_ in range(NKV):
                    pb = k_ % 2
                    L = [wkv[:, c_ * 1024 + k_ * 128:c_ * 1024 + (k_ + 1) * 128]
                         for c_ in range(16)]
                    R = [h1T3[:, c_, :] for c_ in range(16)]

                    def ev(v, ps, k_=k_, grp=grp):
                        return v.tensor_copy(
                            kT[:, k_ * S + grp * 512:k_ * S + (grp + 1) * 512], ps[:, :])
                    emit_mm_evac(L, R, pb, ev, waits=[(vS, kvw), (dKV, wkv_rdy)])
                for tsub in range(4):
                    pb = tsub % 2
                    L = [h1T3[:, c_, tsub * 128:(tsub + 1) * 128] for c_ in range(16)]
                    R = [wkv[:, c_ * 1024 + 512:(c_ + 1) * 1024] for c_ in range(16)]
                    stt = grp * 4 + tsub

                    def ev(v, ps, stt=stt):
                        return v.tensor_copy(vtm[:, stt * 512:(stt + 1) * 512], ps[:, :])
                    emit_mm_evac(L, R, pb, ev, waits=[(vS, kvw), (dKV, wkv_rdy)])
                kv_gate = tS.n

            kT_done = vS.n
            for k_ in range(NKV):
                for ch in range(8):
                    idx = k_ * 8 + ch

                    @em("scalar")
                    def _(c, k_=k_, ch=ch, w=kT_done, sq_gate=tS.n):
                        c.wait_ge(vS.h, w)
                        c.wait_ge(tS.h, sq_gate)
                        c.activation(sqb, kT[:, k_ * S + ch * 512:k_ * S + (ch + 1) * 512],
                                     AF.Square).then_inc(cS.h, 1)
                    cS.n += 1
                    sqv = cS.n

                    @em("tensor")
                    def _(t, sqv=sqv, pf=psfree[2]):
                        t.wait_ge(cS.h, sqv)
                        t.wait_ge(vS.h, pf)
                        t.matmul(psf[2][:, :], ones128[:, :], sqb, start=True,
                                 stop=True).then_inc(tS.h, 1)
                    tS.n += 1
                    mmv = tS.n

                    @em("vector")
                    def _(v, mmv=mmv, idx=idx):
                        v.wait_ge(tS.h, mmv)
                        v.reduce_max(sml[0:1, 40 + idx:41 + idx], psf[2][0:1, :],
                                     axis=AX.X).then_inc(vS.h, 1)
                    vS.n += 1
                    psfree[2] = vS.n

            @em("vector")
            def _(v, w=vS.n):
                v.wait_ge(vS.h, w)
                v.reduce_max(sml[0:1, kmax_col:kmax_col + 1], sml[0:1, 40:72],
                             axis=AX.X).then_inc(vS.h, 1)
            vS.n += 1

            if "d_kT" in dbg:
                @em("sync")
                def _(s, w=vS.n):
                    s.wait_ge(vS.h, w)
                    s.dma_start(dbg["d_kT"], kT[:, :]).then_inc(dDbg.h, 16)
                dDbg.n += 16
            if "d_v" in dbg:
                @em("sync")
                def _(s, w=vS.n):
                    s.wait_ge(vS.h, w)
                    s.dma_start(dbg["d_v"], vtm[:, :]).then_inc(dDbg.h, 16)
                dDbg.n += 16
            if "d_h1T" in dbg:
                @em("sync")
                def _(s, w=vS.n):
                    s.wait_ge(vS.h, w)
                    s.dma_start(dbg["d_h1T"], h1Tb[:, :]).then_inc(dDbg.h, 16)
                dDbg.n += 16
            if "d_sml" in dbg:
                @em("sync")
                def _(s, w=vS.n, wc=cS.n):
                    s.wait_ge(vS.h, w)
                    s.wait_ge(cS.h, wc)
                    s.dma_start(dbg["d_sml"], sml[:, :]).then_inc(dDbg.h, 16)
                dDbg.n += 16
            if "d_h1b" in dbg:
                @em("sync")
                def _(s, w=cS.n):
                    s.wait_ge(cS.h, w)
                    s.dma_start(dbg["d_h1b"][:, 0:2048], h1b[0]).then_inc(dDbg.h, 16)
                    s.dma_start(dbg["d_h1b"][:, 2048:4096], h1b[1]).then_inc(dDbg.h, 16)
                dDbg.n += 32
            if "d_xst" in dbg:
                @em("sync")
                def _(s, w=cS.n):
                    s.wait_ge(cS.h, w)
                    s.dma_start(dbg["d_xst"][:, 0:2048], xst[0]).then_inc(dDbg.h, 16)
                    s.dma_start(dbg["d_xst"][:, 2048:4096], xst[1]).then_inc(dDbg.h, 16)
                dDbg.n += 32
            barrier()

        if "PBC" in phases:
            wprfree = [0, 0]
            xownfree = [0, 0]
            qpr_gate = [0]
            for og in range(2):
                h1free = [tS.n, tS.n]
                for tl in range(4):
                    ti = og * 4 + tl
                    sl = tl % 2

                    @em("sync")
                    def _(s, ti=ti, sl=sl, fv=xownfree[sl]):
                        s.wait_ge(cS.h, fv)
                        s.dma_start(resid[:, ti * 2048:(ti + 1) * 2048],
                                    x_own[ti * 128:(ti + 1) * 128, :]).then_inc(dX[sl].h, 16)
                    dX[sl].n += 16
                    h1v = emit_ln(resid[:, ti * 2048:(ti + 1) * 2048], h1b[sl], 120,
                                  (dX[sl], dX[sl].n), (tS, h1free[sl]), sl)
                    xownfree[sl] = h1v
                    trv, _ = emit_transpose_tile(h1b[sl], h1T3, tl, h1v)
                    h1free[sl] = trv
                h1T_done = vS.n

                for qs in range(4):
                    wsl = qs % 2

                    @em("gpsimd")
                    def _(g, qs=qs, wsl=wsl, wf=wprfree[wsl]):
                        g.wait_ge(tS.h, wf)
                        for c_ in range(16):
                            g.dma_start(wpr3[:, wsl, c_, :],
                                        wq_t[c_ * 128:(c_ + 1) * 128,
                                             qs * 512:(qs + 1) * 512]).then_inc(dWp[wsl].h, 16)
                    dWp[wsl].n += 256
                    wrdy = dWp[wsl].n
                    for hsub in range(4):
                        head = qs * 4 + hsub
                        pb = hsub % 2
                        L = [wpr3[:, wsl, c_, hsub * 128:(hsub + 1) * 128] for c_ in range(16)]
                        R = [h1T3[:, c_, :] for c_ in range(16)]

                        def ev(v, ps, head=head):
                            return v.tensor_copy(qT3[:, head, :], ps[:, :])
                        def ev2(v, ps, head=head):
                            return v.tensor_single_scalar(qT3[:, head, :], ps[:, :],
                                                          SCALE, ALU.mult)
                        emit_mm_evac(L, R, pb, ev2,
                                     waits=[(vS, h1T_done), (dWp[wsl], wrdy)])
                    wprfree[wsl] = tS.n
                qT_done = vS.n
                emit_prep(24)

                for head in range(NHEAD):
                    @em("scalar")
                    def _(c, head=head, w=qT_done, gate=tS.n):
                        c.wait_ge(vS.h, w)
                        c.wait_ge(tS.h, gate)
                        c.activation(sqb, qT3[:, head, :], AF.Square).then_inc(cS.h, 1)
                    cS.n += 1
                    sqv = cS.n

                    @em("tensor")
                    def _(t, sqv=sqv, pf=psfree[2]):
                        t.wait_ge(cS.h, sqv)
                        t.wait_ge(vS.h, pf)
                        t.matmul(psf[2][:, :], ones128[:, :], sqb, start=True,
                                 stop=True).then_inc(tS.h, 1)
                    tS.n += 1
                    mmv = tS.n

                    @em("vector")
                    def _(v, mmv=mmv, head=head):
                        v.wait_ge(tS.h, mmv)
                        v.reduce_max(sml[0:1, 80 + head:81 + head], psf[2][0:1, :],
                                     axis=AX.X).then_inc(vS.h, 1)
                    vS.n += 1
                    psfree[2] = vS.n

                @em("vector")
                def _(v, w=vS.n):
                    v.wait_ge(vS.h, w)
                    v.reduce_max(sml[0:1, 96:97], sml[0:1, 80:96], axis=AX.X).then_inc(vS.h, 1)
                vS.n += 1
                qmx = vS.n

                @em("scalar")
                def _(c, w=qmx):
                    c.wait_ge(vS.h, w)
                    c.activation(sml[0:1, 97:98], sml[0:1, 96:97], AF.Ln)
                    c.activation(sml[0:1, 98:99], sml[0:1, kmax_col:kmax_col + 1],
                                 AF.Ln).then_inc(cS.h, 1)
                cS.n += 1
                lns = cS.n

                @em("vector")
                def _(v, w=lns):
                    v.wait_ge(cS.h, w)
                    v.tensor_scalar(sml[0:1, 99:100], sml[0:1, 97:98],
                                    sml[0:1, 98:99], None, ALU.add,
                                    ALU.bypass).then_inc(vS.h, 1)
                vS.n += 1
                addv = vS.n

                @em("scalar")
                def _(c, w=addv):
                    c.wait_ge(vS.h, w)
                    c.activation(sml[0:1, 100:101], sml[0:1, 99:100], AF.Exp,
                                 scale=0.5).then_inc(cS.h, 1)
                cS.n += 1
                bv = cS.n

                @em("vector")
                def _(v, w=bv):
                    v.wait_ge(cS.h, w)
                    v.tensor_single_scalar(sml[0:1, 101:102], sml[0:1, 100:101], -1.02,
                                           ALU.mult).then_inc(vS.h, 1)
                vS.n += 1
                nbv = vS.n

                @em("sync")
                def _(s, og=og, qg=qpr_gate[0]):
                    s.wait_ge(tS.h, qg)
                    s.dma_start(qpr[0:1, :],
                                qpos_t[0:1, og * 512:og * 512 + 256]).then_inc(dQp.h, 16)
                dQp.n += 16
                qprv = dQp.n

                @em("tensor")
                def _(t, w=nbv, qprv=qprv, pf2=psfree[2], pf3=psfree[3]):
                    t.wait_ge(vS.h, max(w, pf2, pf3))
                    t.wait_ge(dQp.h, qprv)
                    t.matmul(psf[2][:, 0:1], onescolf[0:1, :], sml[0:1, 101:102],
                             start=True, stop=True)
                    t.matmul(psf[3][:, 0:256], onescolf[0:1, :], qpr[0:1, :],
                             start=True, stop=True).then_inc(tS.h, 1)
                tS.n += 1
                bc1 = tS.n

                @em("sync")
                def _(s, og=og, bc1=bc1):
                    s.wait_ge(tS.h, bc1)
                    s.dma_start(qpr[0:1, :],
                                qpos_t[0:1, og * 512 + 256:(og + 1) * 512]).then_inc(dQp.h, 16)
                dQp.n += 16
                qprv2 = dQp.n

                @em("tensor")
                def _(t, qprv2=qprv2):
                    t.wait_ge(dQp.h, qprv2)
                    t.matmul(psf[3][:, 256:512], onescolf[0:1, :], qpr[0:1, :],
                             start=True, stop=True).then_inc(tS.h, 1)
                tS.n += 1
                bcv = tS.n
                qpr_gate[0] = bcv

                @em("vector")
                def _(v, w=bcv):
                    v.wait_ge(tS.h, w)
                    v.tensor_copy(sml[:, 106:107], psf[2][:, 0:1])
                    v.tensor_copy(qposB, psf[3][:, :]).then_inc(vS.h, 1)
                vS.n += 1
                psfree[2] = vS.n
                psfree[3] = vS.n
                setup_done_v = vS.n

                if "d_qT" in dbg and og == 0:
                    @em("sync")
                    def _(s, w=qT_done):
                        s.wait_ge(vS.h, w)
                        s.dma_start(dbg["d_qT"], qTb[:, :]).then_inc(dDbg.h, 16)
                    dDbg.n += 16

                for kv in range(NKV):
                    for pr in range(2):
                        emit_prep(15)
                        h0 = kv * 4 + pr * 2
                        prev_mkv = 0
                        expfree = [0, 0]
                        expdone = [0]
                        scv_tab = {}
                        first_gate = max(setup_done_v, psfree[0], psfree[1],
                                         psfree[2], psfree[3], psfree[4], psfree[5])

                        def emit_scores(sb, kv=kv, h0=h0):
                            @em("tensor")
                            def _(t, sb=sb, kv=kv, h0=h0, scf=expdone[0],
                                  fe=(first_gate if sb == 0 else 0)):
                                if fe:
                                    t.wait_ge(vS.h, fe)
                                t.wait_ge(cS.h, scf)
                                t.matmul(psw[:, 0:512],
                                         kT[:, kv * S + sb * 128:kv * S + (sb + 1) * 128],
                                         qT3[:, h0, :], start=True, stop=True)
                                t.matmul(psw[:, 512:1024],
                                         kT[:, kv * S + sb * 128:kv * S + (sb + 1) * 128],
                                         qT3[:, h0 + 1, :], start=True,
                                         stop=True).then_inc(tS.h, 1)
                            tS.n += 1
                            scv_tab[sb] = tS.n

                        emit_scores(0)
                        NSB = 16 if og == 0 else 32
                        for sb in range(NSB):
                            @em("scalar")
                            def _(c, sb=sb, scv=scv_tab[sb], ef=expfree[sb % 2]):
                                c.wait_ge(tS.h, scv)
                                if ef:
                                    c.wait_ge(tS.h, ef)
                                c.activation(expw[sb % 2][:, :], psw[:, :], AF.Exp,
                                             bias=sml[:, 106:107]).then_inc(cS.h, 1)
                            cS.n += 1
                            expv = cS.n
                            expdone[0] = expv

                            masked = not (og == 1 and sb < 16)
                            if masked:
                                @em("vector")
                                def _(v, sb=sb, expv=expv, base=vS.n, qrdy=setup_done_v,
                                      pmk=prev_mkv):
                                    if sb == 0:
                                        v.wait_ge(vS.h, qrdy)
                                    elif pmk:
                                        v.wait_ge(vS.h, pmk)
                                    v.tensor_single_scalar(maskb[0], qposB,
                                                           sposall[:, sb:sb + 1],
                                                           ALU.is_ge).then_inc(vS.h, 1)
                                    v.wait_ge(vS.h, base + 1)
                                    v.wait_ge(cS.h, expv)
                                    v.tensor_mul(expb[sb % 2][0], expb[sb % 2][0],
                                                 maskb[0])
                                    v.tensor_mul(expb[sb % 2][1], expb[sb % 2][1],
                                                 maskb[0]).then_inc(vS.h, 1)
                                vS.n += 2
                                mkv = vS.n
                                prev_mkv = mkv
                            else:
                                mkv = 0

                            if sb < NSB - 1:
                                emit_scores(sb + 1)

                            @em("tensor")
                            def _(t, kv=kv, sb=sb, mkv=mkv, expv=expv, nsb=NSB):
                                if mkv:
                                    t.wait_ge(vS.h, mkv)
                                else:
                                    t.wait_ge(cS.h, expv)
                                t.matmul(psf[2][:, :],
                                         vtm[:, sb * 512 + kv * 128:sb * 512 + (kv + 1) * 128],
                                         expb[sb % 2][0], start=(sb == 0), stop=(sb == nsb - 1))
                                t.matmul(psf[3][:, :],
                                         vtm[:, sb * 512 + kv * 128:sb * 512 + (kv + 1) * 128],
                                         expb[sb % 2][1], start=(sb == 0), stop=(sb == nsb - 1))
                                t.matmul(psf[4][:, :], ones128[:, :], expb[sb % 2][0],
                                         start=(sb == 0), stop=(sb == nsb - 1))
                                t.matmul(psf[5][:, :], ones128[:, :], expb[sb % 2][1],
                                         start=(sb == 0), stop=(sb == nsb - 1)).then_inc(tS.h, 1)
                            tS.n += 1
                            expfree[sb % 2] = tS.n
                        avsum_done = tS.n

                        @em("scalar")
                        def _(c, w=avsum_done, base=cS.n):
                            c.wait_ge(tS.h, w)
                            c.activation(recipb[0], psf[4][:, :], AF.Ln)
                            c.activation(recipb[1], psf[5][:, :], AF.Ln).then_inc(cS.h, 1)
                            c.wait_ge(cS.h, base + 1)
                            c.activation(recipb[0], recipb[0], AF.Exp, scale=-1.0)
                            c.activation(recipb[1], recipb[1], AF.Exp,
                                         scale=-1.0).then_inc(cS.h, 1)
                        cS.n += 2
                        rcv = cS.n

                        @em("vector")
                        def _(v, h0=h0, rcv=rcv):
                            v.wait_ge(cS.h, rcv)
                            v.tensor_tensor(aoT3[:, h0, :], psf[2][:, :], recipb[0], ALU.mult)
                            v.tensor_tensor(aoT3[:, h0 + 1, :], psf[3][:, :], recipb[1],
                                            ALU.mult).then_inc(vS.h, 1)
                        vS.n += 1
                        for b_ in (2, 3, 4, 5):
                            psfree[b_] = vS.n
                        setup_done_v = vS.n

                if "d_spos" in dbg and og == 0:
                    @em("sync")
                    def _(s, w=vS.n):
                        s.wait_ge(vS.h, w)
                        s.dma_start(dbg["d_spos"], sposall[:, :]).then_inc(dDbg.h, 16)
                        s.dma_start(dbg["d_qposB"], qposB).then_inc(dDbg.h, 16)
                        s.dma_start(dbg["d_mask"], maskb[0]).then_inc(dDbg.h, 16)
                    dDbg.n += 48
                if "d_aoT" in dbg and og == 0:
                    @em("sync")
                    def _(s, w=vS.n):
                        s.wait_ge(vS.h, w)
                        s.dma_start(dbg["d_aoT"], h1Tb[:, :]).then_inc(dDbg.h, 16)
                    dDbg.n += 16

                ao_done = vS.n
                for jq in range(4):
                    wsl = jq % 2

                    @em("gpsimd")
                    def _(g, jq=jq, wsl=wsl, wf=wprfree[wsl]):
                        g.wait_ge(tS.h, wf)
                        for c_ in range(16):
                            g.dma_start(wpr3[:, wsl, c_, :],
                                        wo_t[c_ * 128:(c_ + 1) * 128,
                                             jq * 512:(jq + 1) * 512]).then_inc(dWp[wsl].h, 16)
                    dWp[wsl].n += 256
                    wrdy = dWp[wsl].n
                    for qsub in range(4):
                        ti = og * 4 + qsub
                        pb = qsub % 2
                        L = [aoT3[:, c_, qsub * 128:(qsub + 1) * 128] for c_ in range(16)]
                        R = [wpr3[:, wsl, c_, :] for c_ in range(16)]

                        def ev(v, ps, ti=ti, jq=jq):
                            dst = resid[:, ti * 2048 + jq * 512:ti * 2048 + (jq + 1) * 512]
                            return v.tensor_tensor(dst, ps[:, :], dst, ALU.add)
                        emit_mm_evac(L, R, pb, ev, waits=[(vS, ao_done), (dWp[wsl], wrdy)])
                    wprfree[wsl] = tS.n
            emit_prep(NPREP)
            if "d_resid" in dbg:
                @em("sync")
                def _(s, w=vS.n):
                    s.wait_ge(vS.h, w)
                    s.dma_start(dbg["d_resid"], resid[:, :]).then_inc(dDbg.h, 16)
                dDbg.n += 16
            barrier()

        if "PE" in phases:
            prev_done = (bar, bar.n)
            for ti in range(8):
                par = ti % 2
                h2v = emit_ln(resid[:, ti * 2048:(ti + 1) * 2048], h2s, 120,
                              prev_done, None, par)
                h3v = emit_ln(h2s, h3s, 121, (cS, h2v), None, par)

                @em("vector")
                def _(v, w=h3v, b0=par * 16):
                    v.wait_ge(cS.h, w)
                    v.reduce_max(sml[:, b0 + 8:b0 + 9], h3s, axis=AX.X,
                                 apply_absolute_value=True).then_inc(vS.h, 1)
                vS.n += 1
                amv = vS.n

                @em("scalar")
                def _(c, w=amv, b0=par * 16, base=cS.n):
                    c.wait_ge(vS.h, w)
                    c.activation(sml[:, b0 + 9:b0 + 10], sml[:, b0 + 8:b0 + 9],
                                 AF.Ln).then_inc(cS.h, 1)
                    c.wait_ge(cS.h, base + 1)
                    c.activation(sml[:, b0 + 10:b0 + 11], sml[:, b0 + 9:b0 + 10], AF.Exp,
                                 scale=-1.0).then_inc(cS.h, 1)
                cS.n += 2
                invv = cS.n

                @em("vector")
                def _(v, w=invv, b0=par * 16):
                    v.wait_ge(cS.h, w)
                    v.tensor_single_scalar(sml[:, b0 + 10:b0 + 11],
                                           sml[:, b0 + 10:b0 + 11], 127.0,
                                           ALU.mult).then_inc(vS.h, 1)
                vS.n += 1
                sv_ = vS.n

                @em("scalar")
                def _(c, w=sv_, b0=par * 16, base=cS.n):
                    c.wait_ge(vS.h, w)
                    c.activation(h2s, h3s, AF.Identity,
                                 scale=sml[:, b0 + 10:b0 + 11])
                    c.activation(sml[:, b0 + 9:b0 + 10], sml[:, b0 + 10:b0 + 11],
                                 AF.Ln).then_inc(cS.h, 1)
                    c.wait_ge(cS.h, base + 1)
                    c.activation(sml[:, b0 + 11:b0 + 12], sml[:, b0 + 9:b0 + 10], AF.Exp,
                                 scale=-1.0).then_inc(cS.h, 1)
                cS.n += 2
                yv = cS.n

                @em("vector")
                def _(v, w=yv, base=vS.n):
                    v.wait_ge(cS.h, w)
                    v.tensor_copy(i16s, h2s).then_inc(vS.h, 1)
                    v.wait_ge(vS.h, base + 1)
                    v.tensor_single_scalar(ifs, i16s, 1.0, ALU.mult).then_inc(vS.h, 1)
                    v.wait_ge(vS.h, base + 2)
                    v.tensor_tensor(h3s, h2s, ifs, ALU.subtract).then_inc(vS.h, 1)
                    v.wait_ge(vS.h, base + 3)
                    v.scalar_tensor_tensor(h2s, h3s, 0.5, ifs, ALU.is_ge,
                                           ALU.add).then_inc(vS.h, 1)
                    v.wait_ge(vS.h, base + 4)
                    v.tensor_scalar(h3s, h3s, -0.5, -1.0, ALU.is_lt,
                                    ALU.mult).then_inc(vS.h, 1)
                    v.wait_ge(vS.h, base + 5)
                    v.tensor_tensor(h2s, h2s, h3s, ALU.add).then_inc(vS.h, 1)
                vS.n += 6
                rdv = vS.n

                @em("scalar")
                def _(c, w=rdv, b0=par * 16):
                    c.wait_ge(vS.h, w)
                    c.activation(xqb, h2s, AF.Identity,
                                 scale=sml[:, b0 + 11:b0 + 12]).then_inc(cS.h, 1)
                cS.n += 1
                xqv = cS.n
                trv, _ = emit_transpose_tile(xqb, xqT3, ti, xqv)
                prev_done = (vS, vS.n)

            @em("gpsimd")
            def _(g, w=prev_done[1]):
                g.wait_ge(vS.h, w)
                for ti in range(8):
                    g.dma_start(scrr[ti * 128:(ti + 1) * 128, :],
                                resid[:, ti * 2048:(ti + 1) * 2048]).then_inc(dEv.h, 16)
            dEv.n += 128
            if "d_xqT" in dbg:
                @em("sync")
                def _(s, w=prev_done[1]):
                    s.wait_ge(vS.h, w)
                    s.dma_start(dbg["d_xqT"], xqT3[:, :, :]).then_inc(dDbg.h, 16)
                dDbg.n += 16
            barrier()

            @em("scalar")
            def _(c):
                c.add_instruction(mybir.InstLoadActFuncSet(
                    name=nc.get_next_instruction_name(), act_func_set_id=18, ins=[], outs=[]))

            for pss in range(2):
                wfree = [0, 0]
                gsfree = [0, 0]
                for hs in range(16):
                    wsl = hs % 2

                    @em("sync")
                    def _(s, hs=hs, wsl=wsl, wf=wfree[wsl]):
                        s.wait_ge(tS.h, wf)
                        for c_ in range(16):
                            s.dma_start_transpose(
                                wg3[:, wsl, c_, :],
                                scrg[hs * 512:(hs + 1) * 512,
                                     c_ * 128:(c_ + 1) * 128]).then_inc(dTR[wsl].h, 16)
                    dTR[wsl].n += 256

                    @em("sync")
                    def _(s, hs=hs, wsl=wsl, wf=wfree[wsl]):
                        s.wait_ge(tS.h, wf)
                        for c_ in range(16):
                            s.dma_start_transpose(
                                wu3[:, wsl, c_, :],
                                scru[hs * 512:(hs + 1) * 512,
                                     c_ * 128:(c_ + 1) * 128]).then_inc(dTR[wsl].h, 16)
                    dTR[wsl].n += 256
                    wrdy = dTR[wsl].n
                    for hsub in range(4):
                        pbg = hsub % 2
                        pbu = 2 + hsub % 2

                        @em("tensor")
                        def _(t, pss=pss, wsl=wsl, hsub=hsub, pbg=pbg, pbu=pbu, wrdy=wrdy,
                              pfg=psfree[pbg], pfu=psfree[pbu]):
                            t.wait_ge(dTR[wsl].h, wrdy)
                            t.wait_ge(vS.h, max(pfg, pfu))
                            for c_ in range(16):
                                ins = t.matmul(psf[pbg][:, :],
                                               wg3[:, wsl, c_, hsub * 128:(hsub + 1) * 128],
                                               xqT3[:, c_, pss * 512:(pss + 1) * 512],
                                               start=(c_ == 0), stop=(c_ == 15))
                            for c_ in range(16):
                                ins = t.matmul(psf[pbu][:, :],
                                               wu3[:, wsl, c_, hsub * 128:(hsub + 1) * 128],
                                               xqT3[:, c_, pss * 512:(pss + 1) * 512],
                                               start=(c_ == 0), stop=(c_ == 15))
                            ins.then_inc(tS.h, 1)
                        tS.n += 1
                        guv = tS.n

                        @em("scalar")
                        def _(c, pbg=pbg, guv=guv, gsl=hsub % 2, gf=gsfree[hsub % 2]):
                            c.wait_ge(tS.h, guv)
                            c.wait_ge(vS.h, gf)
                            c.activation(gsb[gsl], psf[pbg][:, :], AF.Silu).then_inc(cS.h, 1)
                        cS.n += 1
                        slv = cS.n

                        @em("vector")
                        def _(v, pss=pss, hs=hs, hsub=hsub, pbu=pbu, slv=slv, gsl=hsub % 2):
                            v.wait_ge(cS.h, slv)
                            slot = hs * 4 + hsub
                            v.tensor_tensor(pT[:, slot * 512:(slot + 1) * 512],
                                            gsb[gsl], psf[pbu][:, :],
                                            ALU.mult).then_inc(vS.h, 1)
                        vS.n += 1
                        psfree[pbg] = vS.n
                        psfree[pbu] = vS.n
                        gsfree[hsub % 2] = vS.n
                    wfree[wsl] = tS.n
                pT_done = vS.n
                if "d_pT" in dbg and pss == 0:
                    @em("sync")
                    def _(s, w=pT_done):
                        s.wait_ge(vS.h, w)
                        s.dma_start(dbg["d_pT"], pT[:, :]).then_inc(dDbg.h, 16)
                    dDbg.n += 16

                outfree = [0, 0]
                @em("sync")
                def _(s, pss=pss):
                    for tl in range(4):
                        ti = pss * 4 + tl
                        s.dma_start(resst[:, tl * 2048:(tl + 1) * 2048],
                                    scrr[ti * 128:(ti + 1) * 128, :]).then_inc(dRes.h, 16)
                dRes.n += 64
                res_rdy = dRes.n

                wd_free = [tS.n, tS.n]
                for jq in range(4):
                    @em("sync")
                    def _(s, jq=jq, wf=wd_free[0]):
                        s.wait_ge(tS.h, wf)
                        for hh in range(32):
                            s.dma_start_transpose(
                                wd3k[:, hh, :],
                                scrd[jq * 512:(jq + 1) * 512,
                                     hh * 128:(hh + 1) * 128]).then_inc(dTR[0].h, 16)
                    dTR[0].n += 512

                    @em("sync")
                    def _(s, jq=jq, wf=wd_free[1]):
                        s.wait_ge(tS.h, wf)
                        for hh in range(32):
                            s.dma_start_transpose(
                                wd3v[:, hh, :],
                                scrd[jq * 512:(jq + 1) * 512,
                                     (32 + hh) * 128:(33 + hh) * 128]).then_inc(dTR[1].h, 16)
                    dTR[1].n += 512

                    for tl in range(4):
                        pb = tl % 2

                        @em("tensor")
                        def _(t, jq=jq, tl=tl, pb=pb, wr0=dTR[0].n, wr1=dTR[1].n,
                              pf=psfree[pb], pd=pT_done):
                            t.wait_ge(dTR[0].h, wr0)
                            t.wait_ge(dTR[1].h, wr1)
                            t.wait_ge(vS.h, max(pf, pd))
                            for hh in range(64):
                                arena = wd3k if hh < 32 else wd3v
                                ins = t.matmul(psf[pb][:, :],
                                               pT[:, hh * 512 + tl * 128:hh * 512 + (tl + 1) * 128],
                                               arena[:, hh % 32, :],
                                               start=(hh == 0), stop=(hh == 63))
                            ins.then_inc(tS.h, 1)
                        tS.n += 1
                        mmv = tS.n

                        @em("vector")
                        def _(v, jq=jq, tl=tl, pb=pb, mmv=mmv, rr=res_rdy, osl=tl % 2,
                              of=outfree[tl % 2]):
                            v.wait_ge(tS.h, mmv)
                            v.wait_ge(dRes.h, rr)
                            v.wait_ge(dOutS[osl].h, of)
                            v.tensor_tensor(outst[osl], psf[pb][:, :],
                                            resst[:, tl * 2048 + jq * 512:
                                                  tl * 2048 + (jq + 1) * 512],
                                            ALU.add).then_inc(vS.h, 1)
                        vS.n += 1
                        psfree[pb] = vS.n
                        ov = vS.n

                        @em("gpsimd")
                        def _(g, pss=pss, jq=jq, tl=tl, ov=ov, osl=tl % 2):
                            g.wait_ge(vS.h, ov)
                            ti = pss * 4 + tl
                            g.dma_start(out_t[ti * 128:(ti + 1) * 128,
                                              jq * 512:(jq + 1) * 512],
                                        outst[osl]).then_inc(dOutS[osl].h, 16)
                        dOutS[tl % 2].n += 16
                        outfree[tl % 2] = dOutS[tl % 2].n
                    wd_free = [tS.n, tS.n]
                barrier()

        barrier()

        with nc.Block() as block:
            block.tensor(lambda t: [f(t) for f in ES["tensor"]])
            block.vector(lambda v: [f(v) for f in ES["vector"]])
            block.scalar(lambda c: [f(c) for f in ES["scalar"]])
            block.gpsimd(lambda g: [f(g) for f in ES["gpsimd"]])
            block.sync(lambda s: [f(s) for f in ES["sync"]])
    return nc


_CACHED = {}


def _get_nc(key=("PA", "PBC", "PE"), debug=()):
    k = (tuple(key), tuple(debug))
    if k not in _CACHED:
        _CACHED[k] = build(key, debug)
    return _CACHED[k]


def kernel(**inputs):
    x = np.ascontiguousarray(inputs["L0_attn_x"])
    wq = np.ascontiguousarray(inputs["L0_attn_Wq"])
    wk = np.ascontiguousarray(inputs["L0_attn_Wk"])
    wv = np.ascontiguousarray(inputs["L0_attn_Wv"])
    wo = np.ascontiguousarray(inputs["L0_attn_Wo"])
    wg = np.ascontiguousarray(inputs["L0_ffn_W_g"])
    wu = np.ascontiguousarray(inputs["L0_ffn_W_u"])
    wd = np.ascontiguousarray(inputs["L0_ffn_W_d"])

    nc = _get_nc()
    in_maps = []
    for c in range(8):
        b, j = c // 4, c % 4
        own = np.concatenate([x[b, j * 512:(j + 1) * 512],
                              x[b, (7 - j) * 512:(8 - j) * 512]], axis=0)
        qpos = np.concatenate([np.arange(j * 512, (j + 1) * 512),
                               np.arange((7 - j) * 512, (8 - j) * 512)]).astype(np.float32)
        in_maps.append({
            "x_full": x[b], "x_own": own, "qpos": qpos[None, :],
            "wq": wq, "wk": wk, "wv": wv, "wo": wo, "wg": wg, "wu": wu, "wd": wd,
        })
    res = run_bass_kernel_spmd(nc, in_maps, core_ids=list(range(8)))
    out = np.empty((B, S, D), dtype=np.float32)
    for c in range(8):
        b, j = c // 4, c % 4
        o = res.results[c]["out"]
        out[b, j * 512:(j + 1) * 512] = o[0:512]
        out[b, (7 - j) * 512:(8 - j) * 512] = o[512:1024]
    return out


# version 11 (final)
# speedup vs baseline: 1.2449x; 1.1665x over previous
import numpy as np
import ml_dtypes
import concourse.bass as bass
import concourse.mybir as mybir
from concourse.bass_utils import run_bass_kernel_spmd
from contextlib import ExitStack

F32 = mybir.dt.float32
BF16 = mybir.dt.bfloat16
I16 = mybir.dt.int16
AF = mybir.ActivationFunctionType
ALU = mybir.AluOpType
AX = mybir.AxisListType

B, S, D, H = 2, 4096, 2048, 8192
NKV, DH, NHEAD = 4, 128, 16
SCALE = float(DH) ** -0.5
TOK = 1024
SG = S // 512
LN127 = float(np.log(127.0))
EPS1, EPS2 = 1e-5, 1e-8


class Ctr:
    def __init__(self, st, nc, name):
        self.h = st.enter_context(nc.semaphore(name))
        self.n = 0


def build(phases=("PA", "PBC", "PE"), debug=()):
    nc = bass.Bass("TRN2", target_bir_lowering=False)
    ES = {e: [] for e in ("tensor", "vector", "scalar", "gpsimd", "sync")}

    def em(eng):
        def reg(f):
            ES[eng].append(f)
            return f
        return reg

    x_full = nc.dram_tensor("x_full", [S, D], F32, kind="ExternalInput").ap()
    x_own = nc.dram_tensor("x_own", [TOK, D], F32, kind="ExternalInput").ap()
    qpos_t = nc.dram_tensor("qpos", [1, TOK], F32, kind="ExternalInput").ap()
    wq_t = nc.dram_tensor("wq", [D, D], F32, kind="ExternalInput").ap()
    wk_t = nc.dram_tensor("wk", [D, 512], F32, kind="ExternalInput").ap()
    wv_t = nc.dram_tensor("wv", [D, 512], F32, kind="ExternalInput").ap()
    wo_t = nc.dram_tensor("wo", [D, D], F32, kind="ExternalInput").ap()
    wg_t = nc.dram_tensor("wg", [H, D], F32, kind="ExternalInput").ap()
    wu_t = nc.dram_tensor("wu", [H, D], F32, kind="ExternalInput").ap()
    wd_t = nc.dram_tensor("wd", [D, H], F32, kind="ExternalInput").ap()
    out_t = nc.dram_tensor("out", [TOK, D], F32, kind="ExternalOutput").ap()
    scrg = nc.dram_tensor("scrg", [131072, 128], BF16, kind="Internal").ap()
    scru = nc.dram_tensor("scru", [131072, 128], BF16, kind="Internal").ap()
    scrd = nc.dram_tensor("scrd", [131072, 128], BF16, kind="Internal").ap()
    scrr = nc.dram_tensor("scrr", [TOK, D], F32, kind="Internal").ap()

    dbg = {}
    dbg_specs = {
        "d_kT": ([128, NKV * S], BF16), "d_v": ([128, 32 * 512], BF16),
        "d_qT": ([128, NHEAD * 512], BF16), "d_resid": ([128, 8 * D], F32),
        "d_xqT": ([128, 16 * TOK], BF16), "d_pT": ([128, 64 * 512], BF16),
        "d_aoT": ([128, NHEAD * 512], BF16), "d_h1T": ([128, 16 * 512], BF16),
        "d_sml": ([128, 128], F32), "d_h1b": ([128, 2 * 2048], BF16),
        "d_xst": ([128, 2 * 2048], F32), "d_spos": ([128, 32], F32),
        "d_qposB": ([128, 512], F32), "d_mask": ([128, 512], BF16),
    }
    for name in debug:
        shape, dt = dbg_specs[name]
        dbg[name] = nc.dram_tensor(name, shape, dt, kind="ExternalOutput").ap()

    st = ExitStack()
    with st:
        ones128 = st.enter_context(nc.sbuf_tensor("ones128", [128, 128], BF16))
        identT = st.enter_context(nc.sbuf_tensor("identT", [128, 128], BF16))
        onescolf = st.enter_context(nc.sbuf_tensor("onescolf", [1, 128], F32))
        sposall = st.enter_context(nc.sbuf_tensor("sposall", [128, 32], F32))
        sml = st.enter_context(nc.sbuf_tensor("smalls", [128, 128], F32))
        qpr = st.enter_context(nc.sbuf_tensor("qpr", [1, 256], F32))

        MBASE = (nc.sbuf_top - 208896) & ~31
        assert MBASE >= nc.sbuf_base + 256, f"auto region too small: {nc.sbuf_base} vs {MBASE}"

        def at(name, shape, dtype, off):
            return nc.alloc_sbuf_tensor_at(name, shape, dtype, offset=MBASE + off)

        RA, RBK, RBV, RC, RD = 0, 65536, 98304, 131072, 163840
        resid = at("resid", [128, 16384], F32, RA)
        pT = at("pT", [128, 32768], BF16, RA)
        kT = at("kT", [128, 16384], BF16, RBK)
        wg3 = at("wg3", [128, 2, 16, 512], BF16, RBK)
        wd3k = at("wd3k", [128, 32, 512], BF16, RBK)
        vtm = at("vtm", [128, 16384], BF16, RBV)
        wu3 = at("wu3", [128, 2, 16, 512], BF16, RBV)
        wd3v = at("wd3v", [128, 32, 512], BF16, RBV)
        wkv = at("wkv", [128, 16384], BF16, RC)
        wpr3 = at("wpr3", [128, 2, 16, 512], BF16, RC)
        xqT3 = at("xqT3", [128, 16, 1024], BF16, RC)
        xqTf = at("xqTf", [128, 16384], BF16, RC)
        xst = [at("xst0", [128, 2048], F32, RD + 0)[:, :], at("xst1", [128, 2048], F32, RD + 8192)[:, :]]
        h1b = [at("h1b0", [128, 2048], BF16, RD + 16384)[:, :],
               at("h1b1", [128, 2048], BF16, RD + 20480)[:, :]]
        h1T3 = at("h1T3", [128, 16, 512], BF16, RD + 24576)
        h1Tb = at("h1Tb", [128, 8192], BF16, RD + 24576)
        pin = [at("pin0", [128, 1024], BF16, RD + 40960)[:, :],
               at("pin1", [128, 1024], BF16, RD + 43008)[:, :]]
        qT3 = at("qT3", [128, 16, 512], BF16, RD + 0)
        qTb = at("qTb", [128, 8192], BF16, RD + 0)
        expw = [at("expw0", [128, 1024], BF16, RD + 16384),
                at("expw1", [128, 1024], BF16, RD + 18432)]
        expb = [[expw[p][:, h * 512:(h + 1) * 512] for h in range(2)] for p in range(2)]
        maskb = [at("maskb0", [128, 512], BF16, RD + 20480)[:, :]]
        sqb = at("sqb", [128, 512], BF16, RD + 21504)[:, :]
        recipb = [at("recip0", [128, 512], F32, RD + 16384)[:, :],
                  at("recip1", [128, 512], F32, RD + 18432)[:, :]]
        qposB = at("qposB", [128, 512], F32, RD + 22528)[:, :]
        aoT3 = h1T3
        h2s = at("h2s", [128, 2048], F32, RD + 0)[:, :]
        h3s = at("h3s", [128, 2048], F32, RD + 8192)[:, :]
        i16s = at("i16s", [128, 2048], I16, RD + 16384)[:, :]
        ifs = at("ifs", [128, 2048], F32, RD + 20480)[:, :]
        xqb = at("xqb", [128, 2048], BF16, RD + 28672)[:, :]
        resst = at("resst", [128, 8192], F32, RD + 0)
        outst = [at("outst0", [128, 512], F32, RD + 32768)[:, :],
                 at("outst1", [128, 512], F32, RD + 34816)[:, :]]
        gsb = [at("gs0", [128, 512], F32, RD + 36864)[:, :],
               at("gs1", [128, 512], F32, RD + 38912)[:, :]]

        psw = st.enter_context(nc.psum_tensor("psw", [128, 1024], F32))
        ps_tr = [st.enter_context(nc.psum_tensor(f"pstr{i}", [128, 512], BF16)) for i in range(2)]
        psf = [psw[:, 0:512], psw[:, 512:1024]] + \
            [st.enter_context(nc.psum_tensor(f"psf{i}", [128, 512], F32)) for i in range(4)]

        dX = [Ctr(st, nc, "dX0"), Ctr(st, nc, "dX1")]
        dKV = Ctr(st, nc, "dKV")
        dPin = [Ctr(st, nc, "dPin0"), Ctr(st, nc, "dPin1")]
        dQp = Ctr(st, nc, "dQp")
        dWp = [Ctr(st, nc, "dWp0"), Ctr(st, nc, "dWp1")]
        dSC = [Ctr(st, nc, "dSC0"), Ctr(st, nc, "dSC1")]
        dTR = [Ctr(st, nc, "dTR0"), Ctr(st, nc, "dTR1")]
        dRes = Ctr(st, nc, "dRes")
        dEv = Ctr(st, nc, "dEv")
        dOutS = [Ctr(st, nc, "dOut0"), Ctr(st, nc, "dOut1")]
        dDbg = Ctr(st, nc, "dDbg")
        tS = Ctr(st, nc, "tS")
        vS = Ctr(st, nc, "vS")
        cS = Ctr(st, nc, "cS")
        gS = Ctr(st, nc, "gS")
        bar = Ctr(st, nc, "bar")
        nbar = [0]

        def barrier():
            nbar[0] += 1
            tgt = 5 * nbar[0]
            gsems = [(c.h, c.n) for c in [dKV, dPin[0], dPin[1], dWp[0], dWp[1], dEv,
                                          dOutS[0], dOutS[1]]]
            ssems = [(c.h, c.n) for c in [dX[0], dX[1], dQp, dTR[0], dTR[1], dRes, dDbg]]
            csems = [(dSC[0].h, dSC[0].n), (dSC[1].h, dSC[1].n)]

            @em("tensor")
            def _(t, tgt=tgt):
                t.drain()
                t.sem_inc(bar.h, 1)
                t.wait_ge(bar.h, tgt)

            @em("vector")
            def _(v, tgt=tgt):
                v.drain()
                v.sem_inc(bar.h, 1)
                v.wait_ge(bar.h, tgt)

            @em("scalar")
            def _(c, tgt=tgt, ws=tuple(csems)):
                c.drain()
                for h, n in ws:
                    c.wait_ge(h, n)
                c.sem_inc(bar.h, 1)
                c.wait_ge(bar.h, tgt)

            @em("gpsimd")
            def _(g, tgt=tgt, ws=tuple(gsems)):
                for h, n in ws:
                    g.wait_ge(h, n)
                g.sem_inc(bar.h, 1)
                g.wait_ge(bar.h, tgt)

            @em("sync")
            def _(s, tgt=tgt, ws=tuple(ssems)):
                for h, n in ws:
                    s.wait_ge(h, n)
                s.sem_inc(bar.h, 1)
                s.wait_ge(bar.h, tgt)
            bar.n = tgt

        @em("gpsimd")
        def _(g):
            g.memset(ones128[:, :], 1.0).then_inc(gS.h, 1)
            g.memset(identT[:, :], 1.0).then_inc(gS.h, 1)
            g.memset(onescolf[0:1, :], 1.0).then_inc(gS.h, 1)
            g.memset(sml[:, 120:121], EPS1).then_inc(gS.h, 1)
            g.memset(sml[:, 121:122], EPS2).then_inc(gS.h, 1)
            g.memset(sml[:, 122:123], LN127).then_inc(gS.h, 1)
            g.memset(sml[:, 123:124], -LN127).then_inc(gS.h, 1)
            g.iota(sposall[:, :], [[128, 32]], channel_multiplier=1,
                   allow_small_or_imprecise_dtypes=True).then_inc(gS.h, 1)
            g.wait_ge(gS.h, 8)
            g.affine_select(out=identT[:, :], in_=identT[:, :], compare_op=ALU.is_equal,
                            fill=0.0, base=0, pattern=[[-1, 128]],
                            channel_multiplier=1).then_inc(gS.h, 1)
            g.wait_ge(gS.h, 9)
        gS.n += 9

        @em("scalar")
        def _(c):
            c.add_instruction(mybir.InstLoadActFuncSet(
                name=nc.get_next_instruction_name(), act_func_set_id=6, ins=[], outs=[]))
        barrier()

        prep_list = []
        for blk in range(H // 128):
            for half in range(2):
                prep_list.append((wg_t, scrg, blk, half))
        for blk in range(H // 128):
            for half in range(2):
                prep_list.append((wu_t, scru, blk, half))
        for blk in range(D // 128):
            for part in range(8):
                prep_list.append((wd_t, scrd, blk, part))
        NPREP = len(prep_list)
        pst = {"i": 0, "free": [0, 0]}

        def emit_prep(n):
            cnt = 0
            while cnt < n and pst["i"] < NPREP:
                i = pst["i"]
                pst["i"] += 1
                cnt += 1
                src_t, dst_t, blk, part = prep_list[i]
                sl = i % 2
                tile = pin[sl]
                freev = pst["free"][sl]

                @em("gpsimd")
                def _(g, src_t=src_t, blk=blk, part=part, tile=tile, freev=freev, sl=sl):
                    g.wait_ge(dSC[sl].h, freev)
                    g.dma_start(tile, src_t[blk * 128:(blk + 1) * 128,
                                            part * 1024:(part + 1) * 1024]).then_inc(dPin[sl].h, 16)
                dPin[sl].n += 16
                inv = dPin[sl].n

                @em("scalar")
                def _(c, tile=tile, inv=inv, sl=sl):
                    c.wait_ge(dPin[sl].h, inv)
                    c.activation(tile, tile, AF.Sign).then_inc(cS.h, 1)
                cS.n += 1
                sgv = cS.n
                dSC[sl].n += 16

                a_ = 16 if dst_t is not scrd else 4
                d4 = dst_t.rearrange("(a b c) p -> a b c p", a=a_, c=512)
                bo, fo = blk // 4, (blk % 4) * 128

                @em("scalar")
                def _(c, d4=d4, bo=bo, fo=fo, part=part, tile=tile, sgv=sgv, sl=sl):
                    c.wait_ge(cS.h, sgv)
                    c.dma_start(
                        d4[bo:bo + 1, part * 8:(part + 1) * 8,
                           fo:fo + 128, :].rearrange("a b c p -> (a c) b p"),
                        tile.rearrange("j (h c) -> j h c", h=8)).then_inc(dSC[sl].h, 16)
                pst["free"][sl] = dSC[sl].n

        def emit_ln(src_ap, dst_ap, eps, rdy, gate, par):
            b0 = par * 16

            @em("scalar")
            def _(c, src_ap=src_ap, dst_ap=dst_ap, rdy=rdy, gate=gate, b0=b0, base0=cS.n):
                if rdy is not None:
                    c.wait_ge(rdy[0].h, rdy[1])
                if gate is not None:
                    c.wait_ge(gate[0].h, gate[1])
                c.activation(dst_ap, src_ap, AF.Square,
                             accum_out=sml[:, b0 + 1:b0 + 2]).then_inc(cS.h, 1)
                c.wait_ge(cS.h, base0 + 1)
                c.activation(dst_ap, src_ap, AF.Copy,
                             accum_out=sml[:, b0 + 0:b0 + 1]).then_inc(cS.h, 1)
            cS.n += 2
            st1 = cS.n

            @em("vector")
            def _(v, st1=st1, b0=b0, base=vS.n):
                v.wait_ge(cS.h, st1)
                v.tensor_single_scalar(sml[:, b0 + 2:b0 + 3], sml[:, b0 + 0:b0 + 1],
                                       1.0 / D, ALU.mult)
                v.tensor_single_scalar(sml[:, b0 + 4:b0 + 5], sml[:, b0 + 1:b0 + 2],
                                       1.0 / D, ALU.mult).then_inc(vS.h, 1)
                v.wait_ge(vS.h, base + 1)
                v.tensor_scalar(sml[:, b0 + 3:b0 + 4], sml[:, b0 + 2:b0 + 3],
                                sml[:, b0 + 2:b0 + 3], -1.0, ALU.mult,
                                ALU.mult).then_inc(vS.h, 1)
                v.wait_ge(vS.h, base + 2)
                v.tensor_scalar(sml[:, b0 + 3:b0 + 4], sml[:, b0 + 4:b0 + 5],
                                sml[:, b0 + 3:b0 + 4], None, ALU.add,
                                ALU.bypass).then_inc(vS.h, 1)
            vS.n += 3
            stv = vS.n

            @em("scalar")
            def _(c, stv=stv, b0=b0, eps=eps, base=cS.n):
                c.wait_ge(vS.h, stv)
                c.activation(sml[:, b0 + 4:b0 + 5], sml[:, b0 + 3:b0 + 4], AF.Ln,
                             bias=sml[:, eps:eps + 1]).then_inc(cS.h, 1)
                c.wait_ge(cS.h, base + 1)
                c.activation(sml[:, b0 + 5:b0 + 6], sml[:, b0 + 4:b0 + 5], AF.Exp,
                             scale=-0.5).then_inc(cS.h, 1)
            cS.n += 2
            rsv = cS.n

            @em("vector")
            def _(v, rsv=rsv, b0=b0):
                v.wait_ge(cS.h, rsv)
                v.tensor_scalar(sml[:, b0 + 6:b0 + 7], sml[:, b0 + 2:b0 + 3],
                                sml[:, b0 + 5:b0 + 6], -1.0, ALU.mult,
                                ALU.mult).then_inc(vS.h, 1)
            vS.n += 1
            nmv = vS.n

            @em("scalar")
            def _(c, src_ap=src_ap, dst_ap=dst_ap, nmv=nmv, b0=b0):
                c.wait_ge(vS.h, nmv)
                c.activation(dst_ap, src_ap, AF.Identity, scale=sml[:, b0 + 5:b0 + 6],
                             bias=sml[:, b0 + 6:b0 + 7]).then_inc(cS.h, 1)
            cS.n += 1
            return cS.n

        trps_free = [0, 0]

        def emit_transpose_tile(src_bf_ap, dst3, tl, h1v, extra_tgate=0):
            for c4 in range(4):
                pfv = trps_free[c4 % 2]

                @em("tensor")
                def _(t, src=src_bf_ap, c4=c4, h1v=h1v, pfv=pfv, eg=extra_tgate):
                    t.wait_ge(cS.h, h1v)
                    t.wait_ge(vS.h, pfv)
                    if eg:
                        t.wait_ge(tS.h, eg)
                    for cc in range(4):
                        c_ = c4 * 4 + cc
                        ins = t.transpose(ps_tr[c4 % 2][:, cc * 128:(cc + 1) * 128],
                                          src[:, c_ * 128:(c_ + 1) * 128], identT[:, :])
                    ins.then_inc(tS.h, 1)
                tS.n += 1
                trv = tS.n

                @em("vector")
                def _(v, c4=c4, tl=tl, trv=trv, dst3=dst3):
                    v.wait_ge(tS.h, trv)
                    v.tensor_copy(dst3[:, c4 * 4:(c4 + 1) * 4, tl * 128:(tl + 1) * 128],
                                  ps_tr[c4 % 2][:, :]).then_inc(vS.h, 1)
                vS.n += 1
                trps_free[c4 % 2] = vS.n
            return tS.n, vS.n

        psfree = [0] * 6

        def emit_mm_evac(lhs_slices, rhs_slices, pb, evac_fn, waits=()):
            @em("tensor")
            def _(t, L=lhs_slices, R=rhs_slices, pb=pb, pf=psfree[pb], waits=tuple(waits)):
                for ctr, val in waits:
                    t.wait_ge(ctr.h, val)
                t.wait_ge(vS.h, pf)
                n = len(L)
                for i_ in range(n):
                    ins = t.matmul(psf[pb][:, :], L[i_], R[i_],
                                   start=(i_ == 0), stop=(i_ == n - 1))
                ins.then_inc(tS.h, 1)
            tS.n += 1
            mmv = tS.n

            @em("vector")
            def _(v, pb=pb, mmv=mmv, evac_fn=evac_fn):
                v.wait_ge(tS.h, mmv)
                evac_fn(v, psf[pb]).then_inc(vS.h, 1)
            vS.n += 1
            psfree[pb] = vS.n
            return vS.n

        kmax_col = 72
        if "PA" in phases:
            @em("gpsimd")
            def _(g):
                for c_ in range(16):
                    g.dma_start(wkv[:, c_ * 1024:c_ * 1024 + 512],
                                wk_t[c_ * 128:(c_ + 1) * 128, :]).then_inc(dKV.h, 16)
                    g.dma_start(wkv[:, c_ * 1024 + 512:(c_ + 1) * 1024],
                                wv_t[c_ * 128:(c_ + 1) * 128, :]).then_inc(dKV.h, 16)
            dKV.n += 512
            wkv_rdy = dKV.n

            xfree = [0, 0]
            h1free = [0, 0]
            kv_gate = 0
            for grp in range(SG):
                emit_prep(12)
                for tl in range(4):
                    ti = grp * 4 + tl
                    sl = ti % 2

                    @em("sync")
                    def _(s, ti=ti, sl=sl, fv=xfree[sl]):
                        s.wait_ge(cS.h, fv)
                        s.dma_start(xst[sl],
                                    x_full[ti * 128:(ti + 1) * 128, :]).then_inc(dX[sl].h, 16)
                    dX[sl].n += 16
                    h1v = emit_ln(xst[sl], h1b[sl], 120, (dX[sl], dX[sl].n),
                                  (tS, h1free[sl]), sl)
                    xfree[sl] = h1v
                    trv, _ = emit_transpose_tile(h1b[sl], h1T3, tl, h1v,
                                                 extra_tgate=(kv_gate if tl == 0 else 0))
                    h1free[sl] = trv
                kvw = vS.n
                for k_ in range(NKV):
                    pb = k_ % 2
                    L = [wkv[:, c_ * 1024 + k_ * 128:c_ * 1024 + (k_ + 1) * 128]
                         for c_ in range(16)]
                    R = [h1T3[:, c_, :] for c_ in range(16)]

                    def ev(v, ps, k_=k_, grp=grp):
                        return v.tensor_copy(
                            kT[:, k_ * S + grp * 512:k_ * S + (grp + 1) * 512], ps[:, :])
                    emit_mm_evac(L, R, pb, ev, waits=[(vS, kvw), (dKV, wkv_rdy)])
                for tsub in range(4):
                    pb = tsub % 2
                    L = [h1T3[:, c_, tsub * 128:(tsub + 1) * 128] for c_ in range(16)]
                    R = [wkv[:, c_ * 1024 + 512:(c_ + 1) * 1024] for c_ in range(16)]
                    stt = grp * 4 + tsub

                    def ev(v, ps, stt=stt):
                        return v.tensor_copy(vtm[:, stt * 512:(stt + 1) * 512], ps[:, :])
                    emit_mm_evac(L, R, pb, ev, waits=[(vS, kvw), (dKV, wkv_rdy)])
                kv_gate = tS.n

            kT_done = vS.n
            for k_ in range(NKV):
                for ch in range(8):
                    idx = k_ * 8 + ch

                    @em("scalar")
                    def _(c, k_=k_, ch=ch, w=kT_done, sq_gate=tS.n):
                        c.wait_ge(vS.h, w)
                        c.wait_ge(tS.h, sq_gate)
                        c.activation(sqb, kT[:, k_ * S + ch * 512:k_ * S + (ch + 1) * 512],
                                     AF.Square).then_inc(cS.h, 1)
                    cS.n += 1
                    sqv = cS.n

                    @em("tensor")
                    def _(t, sqv=sqv, pf=psfree[2]):
                        t.wait_ge(cS.h, sqv)
                        t.wait_ge(vS.h, pf)
                        t.matmul(psf[2][:, :], ones128[:, :], sqb, start=True,
                                 stop=True).then_inc(tS.h, 1)
                    tS.n += 1
                    mmv = tS.n

                    @em("vector")
                    def _(v, mmv=mmv, idx=idx):
                        v.wait_ge(tS.h, mmv)
                        v.reduce_max(sml[0:1, 40 + idx:41 + idx], psf[2][0:1, :],
                                     axis=AX.X).then_inc(vS.h, 1)
                    vS.n += 1
                    psfree[2] = vS.n

            @em("vector")
            def _(v, w=vS.n):
                v.wait_ge(vS.h, w)
                v.reduce_max(sml[0:1, kmax_col:kmax_col + 1], sml[0:1, 40:72],
                             axis=AX.X).then_inc(vS.h, 1)
            vS.n += 1

            if "d_kT" in dbg:
                @em("sync")
                def _(s, w=vS.n):
                    s.wait_ge(vS.h, w)
                    s.dma_start(dbg["d_kT"], kT[:, :]).then_inc(dDbg.h, 16)
                dDbg.n += 16
            if "d_v" in dbg:
                @em("sync")
                def _(s, w=vS.n):
                    s.wait_ge(vS.h, w)
                    s.dma_start(dbg["d_v"], vtm[:, :]).then_inc(dDbg.h, 16)
                dDbg.n += 16
            if "d_h1T" in dbg:
                @em("sync")
                def _(s, w=vS.n):
                    s.wait_ge(vS.h, w)
                    s.dma_start(dbg["d_h1T"], h1Tb[:, :]).then_inc(dDbg.h, 16)
                dDbg.n += 16
            if "d_sml" in dbg:
                @em("sync")
                def _(s, w=vS.n, wc=cS.n):
                    s.wait_ge(vS.h, w)
                    s.wait_ge(cS.h, wc)
                    s.dma_start(dbg["d_sml"], sml[:, :]).then_inc(dDbg.h, 16)
                dDbg.n += 16
            if "d_h1b" in dbg:
                @em("sync")
                def _(s, w=cS.n):
                    s.wait_ge(cS.h, w)
                    s.dma_start(dbg["d_h1b"][:, 0:2048], h1b[0]).then_inc(dDbg.h, 16)
                    s.dma_start(dbg["d_h1b"][:, 2048:4096], h1b[1]).then_inc(dDbg.h, 16)
                dDbg.n += 32
            if "d_xst" in dbg:
                @em("sync")
                def _(s, w=cS.n):
                    s.wait_ge(cS.h, w)
                    s.dma_start(dbg["d_xst"][:, 0:2048], xst[0]).then_inc(dDbg.h, 16)
                    s.dma_start(dbg["d_xst"][:, 2048:4096], xst[1]).then_inc(dDbg.h, 16)
                dDbg.n += 32
            barrier()

        if "PBC" in phases:
            wprfree = [0, 0]
            xownfree = [0, 0]
            qpr_gate = [0]
            for og in range(2):
                h1free = [tS.n, tS.n]
                for tl in range(4):
                    ti = og * 4 + tl
                    sl = tl % 2

                    @em("sync")
                    def _(s, ti=ti, sl=sl, fv=xownfree[sl]):
                        s.wait_ge(cS.h, fv)
                        s.dma_start(resid[:, ti * 2048:(ti + 1) * 2048],
                                    x_own[ti * 128:(ti + 1) * 128, :]).then_inc(dX[sl].h, 16)
                    dX[sl].n += 16
                    h1v = emit_ln(resid[:, ti * 2048:(ti + 1) * 2048], h1b[sl], 120,
                                  (dX[sl], dX[sl].n), (tS, h1free[sl]), sl)
                    xownfree[sl] = h1v
                    trv, _ = emit_transpose_tile(h1b[sl], h1T3, tl, h1v)
                    h1free[sl] = trv
                h1T_done = vS.n

                for qs in range(4):
                    wsl = qs % 2

                    @em("gpsimd")
                    def _(g, qs=qs, wsl=wsl, wf=wprfree[wsl]):
                        g.wait_ge(tS.h, wf)
                        for c_ in range(16):
                            g.dma_start(wpr3[:, wsl, c_, :],
                                        wq_t[c_ * 128:(c_ + 1) * 128,
                                             qs * 512:(qs + 1) * 512]).then_inc(dWp[wsl].h, 16)
                    dWp[wsl].n += 256
                    wrdy = dWp[wsl].n
                    for hsub in range(4):
                        head = qs * 4 + hsub
                        pb = hsub % 2
                        L = [wpr3[:, wsl, c_, hsub * 128:(hsub + 1) * 128] for c_ in range(16)]
                        R = [h1T3[:, c_, :] for c_ in range(16)]

                        def ev(v, ps, head=head):
                            return v.tensor_copy(qT3[:, head, :], ps[:, :])
                        def ev2(v, ps, head=head):
                            return v.tensor_single_scalar(qT3[:, head, :], ps[:, :],
                                                          SCALE, ALU.mult)
                        emit_mm_evac(L, R, pb, ev2,
                                     waits=[(vS, h1T_done), (dWp[wsl], wrdy)])
                    wprfree[wsl] = tS.n
                qT_done = vS.n
                emit_prep(24)

                for head in range(NHEAD):
                    @em("scalar")
                    def _(c, head=head, w=qT_done, gate=tS.n):
                        c.wait_ge(vS.h, w)
                        c.wait_ge(tS.h, gate)
                        c.activation(sqb, qT3[:, head, :], AF.Square).then_inc(cS.h, 1)
                    cS.n += 1
                    sqv = cS.n

                    @em("tensor")
                    def _(t, sqv=sqv, pf=psfree[2]):
                        t.wait_ge(cS.h, sqv)
                        t.wait_ge(vS.h, pf)
                        t.matmul(psf[2][:, :], ones128[:, :], sqb, start=True,
                                 stop=True).then_inc(tS.h, 1)
                    tS.n += 1
                    mmv = tS.n

                    @em("vector")
                    def _(v, mmv=mmv, head=head):
                        v.wait_ge(tS.h, mmv)
                        v.reduce_max(sml[0:1, 80 + head:81 + head], psf[2][0:1, :],
                                     axis=AX.X).then_inc(vS.h, 1)
                    vS.n += 1
                    psfree[2] = vS.n

                @em("vector")
                def _(v, w=vS.n):
                    v.wait_ge(vS.h, w)
                    v.reduce_max(sml[0:1, 96:97], sml[0:1, 80:96], axis=AX.X).then_inc(vS.h, 1)
                vS.n += 1
                qmx = vS.n

                @em("scalar")
                def _(c, w=qmx):
                    c.wait_ge(vS.h, w)
                    c.activation(sml[0:1, 97:98], sml[0:1, 96:97], AF.Ln)
                    c.activation(sml[0:1, 98:99], sml[0:1, kmax_col:kmax_col + 1],
                                 AF.Ln).then_inc(cS.h, 1)
                cS.n += 1
                lns = cS.n

                @em("vector")
                def _(v, w=lns):
                    v.wait_ge(cS.h, w)
                    v.tensor_scalar(sml[0:1, 99:100], sml[0:1, 97:98],
                                    sml[0:1, 98:99], None, ALU.add,
                                    ALU.bypass).then_inc(vS.h, 1)
                vS.n += 1
                addv = vS.n

                @em("scalar")
                def _(c, w=addv):
                    c.wait_ge(vS.h, w)
                    c.activation(sml[0:1, 100:101], sml[0:1, 99:100], AF.Exp,
                                 scale=0.5).then_inc(cS.h, 1)
                cS.n += 1
                bv = cS.n

                @em("vector")
                def _(v, w=bv):
                    v.wait_ge(cS.h, w)
                    v.tensor_single_scalar(sml[0:1, 101:102], sml[0:1, 100:101], -1.02,
                                           ALU.mult).then_inc(vS.h, 1)
                vS.n += 1
                nbv = vS.n

                @em("sync")
                def _(s, og=og, qg=qpr_gate[0]):
                    s.wait_ge(tS.h, qg)
                    s.dma_start(qpr[0:1, :],
                                qpos_t[0:1, og * 512:og * 512 + 256]).then_inc(dQp.h, 16)
                dQp.n += 16
                qprv = dQp.n

                @em("tensor")
                def _(t, w=nbv, qprv=qprv, pf2=psfree[2], pf3=psfree[3]):
                    t.wait_ge(vS.h, max(w, pf2, pf3))
                    t.wait_ge(dQp.h, qprv)
                    t.matmul(psf[2][:, 0:1], onescolf[0:1, :], sml[0:1, 101:102],
                             start=True, stop=True)
                    t.matmul(psf[3][:, 0:256], onescolf[0:1, :], qpr[0:1, :],
                             start=True, stop=True).then_inc(tS.h, 1)
                tS.n += 1
                bc1 = tS.n

                @em("sync")
                def _(s, og=og, bc1=bc1):
                    s.wait_ge(tS.h, bc1)
                    s.dma_start(qpr[0:1, :],
                                qpos_t[0:1, og * 512 + 256:(og + 1) * 512]).then_inc(dQp.h, 16)
                dQp.n += 16
                qprv2 = dQp.n

                @em("tensor")
                def _(t, qprv2=qprv2):
                    t.wait_ge(dQp.h, qprv2)
                    t.matmul(psf[3][:, 256:512], onescolf[0:1, :], qpr[0:1, :],
                             start=True, stop=True).then_inc(tS.h, 1)
                tS.n += 1
                bcv = tS.n
                qpr_gate[0] = bcv

                @em("vector")
                def _(v, w=bcv):
                    v.wait_ge(tS.h, w)
                    v.tensor_copy(sml[:, 106:107], psf[2][:, 0:1])
                    v.tensor_copy(qposB, psf[3][:, :]).then_inc(vS.h, 1)
                vS.n += 1
                psfree[2] = vS.n
                psfree[3] = vS.n
                setup_done_v = vS.n

                if "d_qT" in dbg and og == 0:
                    @em("sync")
                    def _(s, w=qT_done):
                        s.wait_ge(vS.h, w)
                        s.dma_start(dbg["d_qT"], qTb[:, :]).then_inc(dDbg.h, 16)
                    dDbg.n += 16

                for kv in range(NKV):
                    for pr in range(2):
                        emit_prep(15)
                        h0 = kv * 4 + pr * 2
                        prev_mkv = 0
                        expfree = [0, 0]
                        expdone = [0]
                        scv_tab = {}
                        first_gate = max(setup_done_v, psfree[0], psfree[1],
                                         psfree[2], psfree[3], psfree[4], psfree[5])

                        def emit_scores(sb, kv=kv, h0=h0):
                            @em("tensor")
                            def _(t, sb=sb, kv=kv, h0=h0, scf=expdone[0],
                                  fe=(first_gate if sb == 0 else 0)):
                                if fe:
                                    t.wait_ge(vS.h, fe)
                                t.wait_ge(cS.h, scf)
                                t.matmul(psw[:, 0:512],
                                         kT[:, kv * S + sb * 128:kv * S + (sb + 1) * 128],
                                         qT3[:, h0, :], start=True, stop=True)
                                t.matmul(psw[:, 512:1024],
                                         kT[:, kv * S + sb * 128:kv * S + (sb + 1) * 128],
                                         qT3[:, h0 + 1, :], start=True,
                                         stop=True).then_inc(tS.h, 1)
                            tS.n += 1
                            scv_tab[sb] = tS.n

                        emit_scores(0)
                        NSB = 16 if og == 0 else 32
                        for sb in range(NSB):
                            @em("scalar")
                            def _(c, sb=sb, scv=scv_tab[sb], ef=expfree[sb % 2]):
                                c.wait_ge(tS.h, scv)
                                if ef:
                                    c.wait_ge(tS.h, ef)
                                c.activation(expw[sb % 2][:, :], psw[:, :], AF.Exp,
                                             bias=sml[:, 106:107]).then_inc(cS.h, 1)
                            cS.n += 1
                            expv = cS.n
                            expdone[0] = expv

                            masked = not (og == 1 and sb < 16)
                            if masked:
                                @em("vector")
                                def _(v, sb=sb, expv=expv, base=vS.n, qrdy=setup_done_v,
                                      pmk=prev_mkv):
                                    if sb == 0:
                                        v.wait_ge(vS.h, qrdy)
                                    elif pmk:
                                        v.wait_ge(vS.h, pmk)
                                    v.tensor_single_scalar(maskb[0], qposB,
                                                           sposall[:, sb:sb + 1],
                                                           ALU.is_ge).then_inc(vS.h, 1)
                                    v.wait_ge(vS.h, base + 1)
                                    v.wait_ge(cS.h, expv)
                                    v.tensor_mul(expb[sb % 2][0], expb[sb % 2][0],
                                                 maskb[0])
                                    v.tensor_mul(expb[sb % 2][1], expb[sb % 2][1],
                                                 maskb[0]).then_inc(vS.h, 1)
                                vS.n += 2
                                mkv = vS.n
                                prev_mkv = mkv
                            else:
                                mkv = 0

                            if sb < NSB - 1:
                                emit_scores(sb + 1)

                            @em("tensor")
                            def _(t, kv=kv, sb=sb, mkv=mkv, expv=expv, nsb=NSB):
                                if mkv:
                                    t.wait_ge(vS.h, mkv)
                                else:
                                    t.wait_ge(cS.h, expv)
                                t.matmul(psf[2][:, :],
                                         vtm[:, sb * 512 + kv * 128:sb * 512 + (kv + 1) * 128],
                                         expb[sb % 2][0], start=(sb == 0), stop=(sb == nsb - 1))
                                t.matmul(psf[3][:, :],
                                         vtm[:, sb * 512 + kv * 128:sb * 512 + (kv + 1) * 128],
                                         expb[sb % 2][1], start=(sb == 0), stop=(sb == nsb - 1))
                                t.matmul(psf[4][:, :], ones128[:, :], expb[sb % 2][0],
                                         start=(sb == 0), stop=(sb == nsb - 1))
                                t.matmul(psf[5][:, :], ones128[:, :], expb[sb % 2][1],
                                         start=(sb == 0), stop=(sb == nsb - 1)).then_inc(tS.h, 1)
                            tS.n += 1
                            expfree[sb % 2] = tS.n
                        avsum_done = tS.n

                        @em("scalar")
                        def _(c, w=avsum_done, base=cS.n):
                            c.wait_ge(tS.h, w)
                            c.activation(recipb[0], psf[4][:, :], AF.Ln)
                            c.activation(recipb[1], psf[5][:, :], AF.Ln).then_inc(cS.h, 1)
                            c.wait_ge(cS.h, base + 1)
                            c.activation(recipb[0], recipb[0], AF.Exp, scale=-1.0)
                            c.activation(recipb[1], recipb[1], AF.Exp,
                                         scale=-1.0).then_inc(cS.h, 1)
                        cS.n += 2
                        rcv = cS.n

                        @em("vector")
                        def _(v, h0=h0, rcv=rcv):
                            v.wait_ge(cS.h, rcv)
                            v.tensor_tensor(aoT3[:, h0, :], psf[2][:, :], recipb[0], ALU.mult)
                            v.tensor_tensor(aoT3[:, h0 + 1, :], psf[3][:, :], recipb[1],
                                            ALU.mult).then_inc(vS.h, 1)
                        vS.n += 1
                        for b_ in (2, 3, 4, 5):
                            psfree[b_] = vS.n
                        setup_done_v = vS.n

                if "d_spos" in dbg and og == 0:
                    @em("sync")
                    def _(s, w=vS.n):
                        s.wait_ge(vS.h, w)
                        s.dma_start(dbg["d_spos"], sposall[:, :]).then_inc(dDbg.h, 16)
                        s.dma_start(dbg["d_qposB"], qposB).then_inc(dDbg.h, 16)
                        s.dma_start(dbg["d_mask"], maskb[0]).then_inc(dDbg.h, 16)
                    dDbg.n += 48
                if "d_aoT" in dbg and og == 0:
                    @em("sync")
                    def _(s, w=vS.n):
                        s.wait_ge(vS.h, w)
                        s.dma_start(dbg["d_aoT"], h1Tb[:, :]).then_inc(dDbg.h, 16)
                    dDbg.n += 16

                ao_done = vS.n
                for jq in range(4):
                    wsl = jq % 2

                    @em("gpsimd")
                    def _(g, jq=jq, wsl=wsl, wf=wprfree[wsl]):
                        g.wait_ge(tS.h, wf)
                        for c_ in range(16):
                            g.dma_start(wpr3[:, wsl, c_, :],
                                        wo_t[c_ * 128:(c_ + 1) * 128,
                                             jq * 512:(jq + 1) * 512]).then_inc(dWp[wsl].h, 16)
                    dWp[wsl].n += 256
                    wrdy = dWp[wsl].n
                    for qsub in range(4):
                        ti = og * 4 + qsub
                        pb = qsub % 2
                        L = [aoT3[:, c_, qsub * 128:(qsub + 1) * 128] for c_ in range(16)]
                        R = [wpr3[:, wsl, c_, :] for c_ in range(16)]

                        def ev(v, ps, ti=ti, jq=jq):
                            dst = resid[:, ti * 2048 + jq * 512:ti * 2048 + (jq + 1) * 512]
                            return v.tensor_tensor(dst, ps[:, :], dst, ALU.add)
                        emit_mm_evac(L, R, pb, ev, waits=[(vS, ao_done), (dWp[wsl], wrdy)])
                    wprfree[wsl] = tS.n
            emit_prep(NPREP)
            if "d_resid" in dbg:
                @em("sync")
                def _(s, w=vS.n):
                    s.wait_ge(vS.h, w)
                    s.dma_start(dbg["d_resid"], resid[:, :]).then_inc(dDbg.h, 16)
                dDbg.n += 16
            barrier()

        if "PE" in phases:
            prev_done = (bar, bar.n)
            for ti in range(8):
                par = ti % 2
                h2v = emit_ln(resid[:, ti * 2048:(ti + 1) * 2048], h2s, 120,
                              prev_done, None, par)
                h3v = emit_ln(h2s, h3s, 121, (cS, h2v), None, par)

                @em("vector")
                def _(v, w=h3v, b0=par * 16):
                    v.wait_ge(cS.h, w)
                    v.reduce_max(sml[:, b0 + 8:b0 + 9], h3s, axis=AX.X,
                                 apply_absolute_value=True).then_inc(vS.h, 1)
                vS.n += 1
                amv = vS.n

                @em("scalar")
                def _(c, w=amv, b0=par * 16, base=cS.n):
                    c.wait_ge(vS.h, w)
                    c.activation(sml[:, b0 + 9:b0 + 10], sml[:, b0 + 8:b0 + 9],
                                 AF.Ln).then_inc(cS.h, 1)
                    c.wait_ge(cS.h, base + 1)
                    c.activation(sml[:, b0 + 10:b0 + 11], sml[:, b0 + 9:b0 + 10], AF.Exp,
                                 scale=-1.0).then_inc(cS.h, 1)
                cS.n += 2
                invv = cS.n

                @em("vector")
                def _(v, w=invv, b0=par * 16):
                    v.wait_ge(cS.h, w)
                    v.tensor_single_scalar(sml[:, b0 + 10:b0 + 11],
                                           sml[:, b0 + 10:b0 + 11], 127.0,
                                           ALU.mult).then_inc(vS.h, 1)
                vS.n += 1
                sv_ = vS.n

                @em("scalar")
                def _(c, w=sv_, b0=par * 16, base=cS.n):
                    c.wait_ge(vS.h, w)
                    c.activation(h2s, h3s, AF.Identity,
                                 scale=sml[:, b0 + 10:b0 + 11])
                    c.activation(sml[:, b0 + 9:b0 + 10], sml[:, b0 + 10:b0 + 11],
                                 AF.Ln).then_inc(cS.h, 1)
                    c.wait_ge(cS.h, base + 1)
                    c.activation(sml[:, b0 + 11:b0 + 12], sml[:, b0 + 9:b0 + 10], AF.Exp,
                                 scale=-1.0).then_inc(cS.h, 1)
                cS.n += 2
                yv = cS.n

                @em("vector")
                def _(v, w=yv, base=vS.n):
                    v.wait_ge(cS.h, w)
                    v.tensor_copy(i16s, h2s).then_inc(vS.h, 1)
                    v.wait_ge(vS.h, base + 1)
                    v.tensor_single_scalar(ifs, i16s, 1.0, ALU.mult).then_inc(vS.h, 1)
                    v.wait_ge(vS.h, base + 2)
                    v.tensor_tensor(h3s, h2s, ifs, ALU.subtract).then_inc(vS.h, 1)
                    v.wait_ge(vS.h, base + 3)
                    v.scalar_tensor_tensor(h2s, h3s, 0.5, ifs, ALU.is_ge,
                                           ALU.add).then_inc(vS.h, 1)
                    v.wait_ge(vS.h, base + 4)
                    v.tensor_scalar(h3s, h3s, -0.5, -1.0, ALU.is_lt,
                                    ALU.mult).then_inc(vS.h, 1)
                    v.wait_ge(vS.h, base + 5)
                    v.tensor_tensor(h2s, h2s, h3s, ALU.add).then_inc(vS.h, 1)
                vS.n += 6
                rdv = vS.n

                @em("scalar")
                def _(c, w=rdv, b0=par * 16):
                    c.wait_ge(vS.h, w)
                    c.activation(xqb, h2s, AF.Identity,
                                 scale=sml[:, b0 + 11:b0 + 12]).then_inc(cS.h, 1)
                cS.n += 1
                xqv = cS.n
                trv, _ = emit_transpose_tile(xqb, xqT3, ti, xqv)
                prev_done = (vS, vS.n)

            @em("gpsimd")
            def _(g, w=prev_done[1]):
                g.wait_ge(vS.h, w)
                for ti in range(8):
                    g.dma_start(scrr[ti * 128:(ti + 1) * 128, :],
                                resid[:, ti * 2048:(ti + 1) * 2048]).then_inc(dEv.h, 16)
            dEv.n += 128
            if "d_xqT" in dbg:
                @em("sync")
                def _(s, w=prev_done[1]):
                    s.wait_ge(vS.h, w)
                    s.dma_start(dbg["d_xqT"], xqT3[:, :, :]).then_inc(dDbg.h, 16)
                dDbg.n += 16
            barrier()

            @em("scalar")
            def _(c):
                c.add_instruction(mybir.InstLoadActFuncSet(
                    name=nc.get_next_instruction_name(), act_func_set_id=18, ins=[], outs=[]))

            for pss in range(2):
                wfree = [0, 0]
                gsfree = [0, 0]
                for hs in range(16):
                    wsl = hs % 2

                    @em("sync")
                    def _(s, hs=hs, wsl=wsl, wf=wfree[wsl]):
                        s.wait_ge(tS.h, wf)
                        s.dma_start_transpose(
                            wg3[:, wsl, :, :].rearrange("p a b -> p (a b)"),
                            scrg[hs * 8192:(hs + 1) * 8192, :]).then_inc(dTR[wsl].h, 16)
                        s.dma_start_transpose(
                            wu3[:, wsl, :, :].rearrange("p a b -> p (a b)"),
                            scru[hs * 8192:(hs + 1) * 8192, :]).then_inc(dTR[wsl].h, 16)
                    dTR[wsl].n += 32
                    wrdy = dTR[wsl].n
                    for hsub in range(4):
                        pbg = hsub % 2
                        pbu = 2 + hsub % 2

                        @em("tensor")
                        def _(t, pss=pss, wsl=wsl, hsub=hsub, pbg=pbg, pbu=pbu, wrdy=wrdy,
                              pfg=psfree[pbg], pfu=psfree[pbu]):
                            t.wait_ge(dTR[wsl].h, wrdy)
                            t.wait_ge(vS.h, max(pfg, pfu))
                            for c_ in range(16):
                                ins = t.matmul(psf[pbg][:, :],
                                               wg3[:, wsl, c_, hsub * 128:(hsub + 1) * 128],
                                               xqT3[:, c_, pss * 512:(pss + 1) * 512],
                                               start=(c_ == 0), stop=(c_ == 15))
                            for c_ in range(16):
                                ins = t.matmul(psf[pbu][:, :],
                                               wu3[:, wsl, c_, hsub * 128:(hsub + 1) * 128],
                                               xqT3[:, c_, pss * 512:(pss + 1) * 512],
                                               start=(c_ == 0), stop=(c_ == 15))
                            ins.then_inc(tS.h, 1)
                        tS.n += 1
                        guv = tS.n

                        @em("scalar")
                        def _(c, pbg=pbg, guv=guv, gsl=hsub % 2, gf=gsfree[hsub % 2]):
                            c.wait_ge(tS.h, guv)
                            c.wait_ge(vS.h, gf)
                            c.activation(gsb[gsl], psf[pbg][:, :], AF.Silu).then_inc(cS.h, 1)
                        cS.n += 1
                        slv = cS.n

                        @em("vector")
                        def _(v, pss=pss, hs=hs, hsub=hsub, pbu=pbu, slv=slv, gsl=hsub % 2):
                            v.wait_ge(cS.h, slv)
                            slot = hs * 4 + hsub
                            v.tensor_tensor(pT[:, slot * 512:(slot + 1) * 512],
                                            gsb[gsl], psf[pbu][:, :],
                                            ALU.mult).then_inc(vS.h, 1)
                        vS.n += 1
                        psfree[pbg] = vS.n
                        psfree[pbu] = vS.n
                        gsfree[hsub % 2] = vS.n
                    wfree[wsl] = tS.n
                pT_done = vS.n
                if "d_pT" in dbg and pss == 0:
                    @em("sync")
                    def _(s, w=pT_done):
                        s.wait_ge(vS.h, w)
                        s.dma_start(dbg["d_pT"], pT[:, :]).then_inc(dDbg.h, 16)
                    dDbg.n += 16

                outfree = [0, 0]
                @em("sync")
                def _(s, pss=pss):
                    for tl in range(4):
                        ti = pss * 4 + tl
                        s.dma_start(resst[:, tl * 2048:(tl + 1) * 2048],
                                    scrr[ti * 128:(ti + 1) * 128, :]).then_inc(dRes.h, 16)
                dRes.n += 64
                res_rdy = dRes.n

                wd_free = [tS.n, tS.n]
                for jq in range(4):
                    @em("sync")
                    def _(s, jq=jq, wf=wd_free[0]):
                        s.wait_ge(tS.h, wf)
                        s.dma_start_transpose(
                            wd3k[:, :, :].rearrange("p a b -> p (a b)"),
                            scrd[jq * 32768:jq * 32768 + 16384, :]).then_inc(dTR[0].h, 16)
                    dTR[0].n += 16

                    @em("sync")
                    def _(s, jq=jq, wf=wd_free[1]):
                        s.wait_ge(tS.h, wf)
                        s.dma_start_transpose(
                            wd3v[:, :, :].rearrange("p a b -> p (a b)"),
                            scrd[jq * 32768 + 16384:(jq + 1) * 32768, :]).then_inc(dTR[1].h, 16)
                    dTR[1].n += 16

                    for tl in range(4):
                        pb = tl % 2

                        @em("tensor")
                        def _(t, jq=jq, tl=tl, pb=pb, wr0=dTR[0].n, wr1=dTR[1].n,
                              pf=psfree[pb], pd=pT_done):
                            t.wait_ge(dTR[0].h, wr0)
                            t.wait_ge(dTR[1].h, wr1)
                            t.wait_ge(vS.h, max(pf, pd))
                            for hh in range(64):
                                arena = wd3k if hh < 32 else wd3v
                                ins = t.matmul(psf[pb][:, :],
                                               pT[:, hh * 512 + tl * 128:hh * 512 + (tl + 1) * 128],
                                               arena[:, hh % 32, :],
                                               start=(hh == 0), stop=(hh == 63))
                            ins.then_inc(tS.h, 1)
                        tS.n += 1
                        mmv = tS.n

                        @em("vector")
                        def _(v, jq=jq, tl=tl, pb=pb, mmv=mmv, rr=res_rdy, osl=tl % 2,
                              of=outfree[tl % 2]):
                            v.wait_ge(tS.h, mmv)
                            v.wait_ge(dRes.h, rr)
                            v.wait_ge(dOutS[osl].h, of)
                            v.tensor_tensor(outst[osl], psf[pb][:, :],
                                            resst[:, tl * 2048 + jq * 512:
                                                  tl * 2048 + (jq + 1) * 512],
                                            ALU.add).then_inc(vS.h, 1)
                        vS.n += 1
                        psfree[pb] = vS.n
                        ov = vS.n

                        @em("gpsimd")
                        def _(g, pss=pss, jq=jq, tl=tl, ov=ov, osl=tl % 2):
                            g.wait_ge(vS.h, ov)
                            ti = pss * 4 + tl
                            g.dma_start(out_t[ti * 128:(ti + 1) * 128,
                                              jq * 512:(jq + 1) * 512],
                                        outst[osl]).then_inc(dOutS[osl].h, 16)
                        dOutS[tl % 2].n += 16
                        outfree[tl % 2] = dOutS[tl % 2].n
                    wd_free = [tS.n, tS.n]
                barrier()

        barrier()

        with nc.Block() as block:
            block.tensor(lambda t: [f(t) for f in ES["tensor"]])
            block.vector(lambda v: [f(v) for f in ES["vector"]])
            block.scalar(lambda c: [f(c) for f in ES["scalar"]])
            block.gpsimd(lambda g: [f(g) for f in ES["gpsimd"]])
            block.sync(lambda s: [f(s) for f in ES["sync"]])
    return nc


_CACHED = {}


def _get_nc(key=("PA", "PBC", "PE"), debug=()):
    k = (tuple(key), tuple(debug))
    if k not in _CACHED:
        _CACHED[k] = build(key, debug)
    return _CACHED[k]


def kernel(**inputs):
    x = np.ascontiguousarray(inputs["L0_attn_x"])
    wq = np.ascontiguousarray(inputs["L0_attn_Wq"])
    wk = np.ascontiguousarray(inputs["L0_attn_Wk"])
    wv = np.ascontiguousarray(inputs["L0_attn_Wv"])
    wo = np.ascontiguousarray(inputs["L0_attn_Wo"])
    wg = np.ascontiguousarray(inputs["L0_ffn_W_g"])
    wu = np.ascontiguousarray(inputs["L0_ffn_W_u"])
    wd = np.ascontiguousarray(inputs["L0_ffn_W_d"])

    nc = _get_nc()
    in_maps = []
    for c in range(8):
        b, j = c // 4, c % 4
        own = np.concatenate([x[b, j * 512:(j + 1) * 512],
                              x[b, (7 - j) * 512:(8 - j) * 512]], axis=0)
        qpos = np.concatenate([np.arange(j * 512, (j + 1) * 512),
                               np.arange((7 - j) * 512, (8 - j) * 512)]).astype(np.float32)
        in_maps.append({
            "x_full": x[b], "x_own": own, "qpos": qpos[None, :],
            "wq": wq, "wk": wk, "wv": wv, "wo": wo, "wg": wg, "wu": wu, "wd": wd,
        })
    res = run_bass_kernel_spmd(nc, in_maps, core_ids=list(range(8)))
    out = np.empty((B, S, D), dtype=np.float32)
    for c in range(8):
        b, j = c // 4, c % 4
        o = res.results[c]["out"]
        out[b, j * 512:(j + 1) * 512] = o[0:512]
        out[b, (7 - j) * 512:(8 - j) * 512] = o[512:1024]
    return out
